# version 01
# baseline (speedup 1.0000x reference)
import numpy as np
import ml_dtypes
from contextlib import ExitStack

import concourse.bass as bass
import concourse.mybir as mybir
from concourse.bass_utils import run_bass_kernel_spmd

F32 = mybir.dt.float32
BF16 = mybir.dt.bfloat16
AF = mybir.ActivationFunctionType
ALU = mybir.AluOpType
AXX = mybir.AxisListType.X

D = 4096
KC = 32
T = 2048
NT = 4
NW = 512
H = 32
HD = 128
BS = 256
NB = 8
E = 8
ER = 128
LN_EPS = 1e-5
RMS_EPS = 1e-5
NCORE = 8

BFnp = ml_dtypes.bfloat16


def _decay_tables():
    h = np.arange(H, dtype=np.float64)
    base = 1.0 / 2.0 ** (8.0 / H)
    factor = 1.0 - 1.0 / (H - 1 + 1e-05) + 1e-05
    slope = (base ** (h + 1.0)) * factor
    r = np.arange(BS, dtype=np.float64) + 1.0
    qd = np.exp(-slope[:, None] * r[None, :])
    kd = np.exp(-slope[:, None] * (BS - r)[None, :])
    diff = r[:, None] - r[None, :]
    mask = np.where(diff >= 0,
                    np.exp(-np.minimum(slope[:, None, None] * diff[None], 500.0)),
                    0.0)
    maskT = mask.transpose(0, 2, 1).copy()
    bd = np.exp(-slope * BS)
    return (qd.astype(np.float32), kd.astype(np.float32),
            maskT.astype(np.float32), bd.astype(np.float32))


QD_TAB, KD_TAB, MASKT_TAB, BD_TAB = _decay_tables()


def _img_lhsT(WT, ocw=128):
    DI, DO = WT.shape
    kc, oc = DI // 128, DO // ocw
    return np.ascontiguousarray(
        WT.reshape(kc, 128, oc, ocw).transpose(1, 2, 0, 3).reshape(128, oc * kc * ocw))


def _img_act(x):
    t, d = x.shape
    return np.ascontiguousarray(
        x.T.reshape(d // 128, 128, t).transpose(1, 0, 2).reshape(128, -1))


ENGS = ("tensor", "vector", "scalar", "gpsimd", "sync")


class Sem:
    def __init__(self, nc, name):
        self.h = nc.alloc_semaphore(name=name)
        self.n = 0


class Prog:
    def __init__(self, nc):
        self.nc = nc
        self.streams = {e: [] for e in ENGS}
        self.last_wait = {e: {} for e in ENGS}
        self.sems = []

    def sem(self, name):
        s = Sem(self.nc, name)
        self.sems.append(s)
        return s

    def emit(self, eng, fn, wait=(), inc=None):
        lw = self.last_wait[eng]
        real = []
        for s, v in wait:
            if v <= 0 or lw.get(s, 0) >= v:
                continue
            assert v <= s.n, f"[{eng}] wait {v} > scheduled {s.n} for sem"
            lw[s] = v
            real.append((s, v))
        if inc is not None:
            inc[0].n += inc[1]
        if fn is None and not real:
            return
        self.streams[eng].append((real, fn, inc))

    def replay(self):
        with self.nc.Block() as block:
            for eng in ENGS:
                stream = self.streams[eng]

                def runner(e, stream=stream):
                    for waits, fn, inc in stream:
                        for s, v in waits:
                            e.wait_ge(s.h, v)
                        if fn is not None:
                            ins = fn(e)
                            if inc is not None:
                                assert ins is not None
                                ins.then_inc(inc[0].h, inc[1])
                getattr(block, eng)(runner)


class Builder:
    def __init__(self, tap=None):
        nc = bass.Bass("TRN2", target_bir_lowering=False)
        self.nc = nc
        self.tap = tap
        self.p = Prog(nc)
        p = self.p

        def ein(name, shape, dt):
            return nc.dram_tensor(name, shape, dt, kind="ExternalInput").ap()

        self.x0_img = ein("x0_img", [128, KC * T], F32)
        self.cc_img = ein("cc_img", [128, NCORE * H], F32)
        self.wq_img = ein("wq_img", [128, H * KC * HD], F32)
        self.wk_img = ein("wk_img", [128, H * KC * HD], F32)
        self.wv_img = ein("wv_img", [128, 8 * KC * 512], F32)
        self.wg_img = ein("wg_img", [128, KC * KC * 128], F32)
        self.wo_img = ein("wo_img", [128, KC * KC * 128], F32)
        self.a_img = ein("a_img", [128, KC * ER], F32)
        self.bw_img = ein("bw_img", [128, KC * 128], F32)
        self.rt_img = ein("rt_img", [128, KC * E], F32)
        self.nw_img = ein("nw_img", [128, H], F32)
        self.qd_img = ein("qd_img", [128, H * BS], F32)
        self.kd_img = ein("kd_img", [128, H * 2], F32)
        self.mk_img = ein("mk_img", [128, H * 2 * BS], F32)
        self.em_img = ein("em_img", [8, ER], BF16)
        self.id16 = ein("id16", [128, 128], BF16)
        self.idf32 = ein("idf32", [128, 128], F32)
        self.id8x = ein("id8x", [128, 8], F32)
        self.onec = ein("onec", [128, 1], BF16)
        self.oner = ein("oner", [1, 128], BF16)

        self.y_img = nc.dram_tensor("y_img", [128, KC * T], F32,
                                    kind="ExternalOutput").ap()
        if tap:
            self.tap_img = nc.dram_tensor("tap_img", [128, KC * T], F32,
                                          kind="ExternalOutput").ap()
            self.tapb_img = nc.dram_tensor("tapb_img", [128, KC * T], BF16,
                                           kind="ExternalOutput").ap()

        self.x1_img = nc.dram_tensor("x1_img", [128, KC * T], F32, kind="Internal").ap()
        self.x2_img = nc.dram_tensor("x2_img", [128, KC * T], F32, kind="Internal").ap()
        self.qt_img = nc.dram_tensor("qt_img", [128, H * T], BF16, kind="Internal").ap()
        self.kt_img = nc.dram_tensor("kt_img", [128, H * T], BF16, kind="Internal").ap()
        self.v_img = nc.dram_tensor("v_img", [128, H * T], BF16, kind="Internal").ap()
        self.at_img = nc.dram_tensor("at_img", [128, H * T], BF16, kind="Internal").ap()
        self.g_img = nc.dram_tensor("g_img", [128, KC * T], BF16, kind="Internal").ap()
        self.u_in = nc.dram_tensor("u_in", [H * HD, HD], F32, kind="Internal").ap()
        self.u_out = nc.dram_tensor("u_out", [NCORE * H * HD, HD], F32,
                                    kind="Internal").ap()

        self.s_wl = p.sem("s_wl")
        self.s_xl = p.sem("s_xl")
        self.s_pe = p.sem("s_pe")
        self.s_dv = p.sem("s_dv")
        self.s_ac = p.sem("s_ac")
        self.s_pl = p.sem("s_pl")
        self.s_st = p.sem("s_st")
        self.s_cc = p.sem("s_cc")

        self.ctx = ExitStack()

    def sb(self, name, shape, dt, cx=None):
        return (cx or self.ctx).enter_context(self.nc.sbuf_tensor(name, shape, dt))

    def psum(self, name, shape, dt=F32, cx=None):
        return (cx or self.ctx).enter_context(self.nc.psum_tensor(name, shape, dt))

    def AB(self, kc, lo=0, hi=T):
        return self.act_buf[:, kc * T + lo: kc * T + hi]

    def fence(self):
        return ((self.s_pe, self.s_pe.n), (self.s_dv, self.s_dv.n),
                (self.s_ac, self.s_ac.n), (self.s_pl, self.s_pl.n),
                (self.s_st, self.s_st.n))

    def alloc_globals(self):
        self.act_buf = self.sb("act_buf", [128, KC * T], BF16)
        self.id16_sb = self.sb("id16_sb", [128, 128], BF16)
        self.idf32_sb = self.sb("idf32_sb", [128, 128], F32)
        self.id8x_sb = self.sb("id8x_sb", [128, 8], F32)
        self.onec_sb = self.sb("onec_sb", [128, 1], BF16)
        self.oner_sb = self.sb("oner_sb", [1, 128], BF16)
        self.nw_sb = self.sb("nw_sb", [128, H], F32)
        self.cc_sb = self.sb("cc_sb", [128, NCORE * H], F32)
        self.kd_sb = self.sb("kd_sb", [128, H * 2], F32)
        self.em_sb = self.sb("em_sb", [8, ER], BF16)

        pairs = ((self.id16_sb, self.id16), (self.idf32_sb, self.idf32), (self.id8x_sb, self.id8x),
                 (self.onec_sb, self.onec), (self.oner_sb, self.oner),
                 (self.nw_sb, self.nw_img), (self.cc_sb, self.cc_img),
                 (self.kd_sb, self.kd_img), (self.em_sb, self.em_img))
        for dst, src in pairs:
            self.p.emit("gpsimd",
                        lambda e, dst=dst, src=src: e.dma_start(out=dst[:, :], in_=src),
                        inc=(self.s_wl, 16))
        self.consts_ready = self.s_wl.n

    def zero_stats(self, ps_sum, ps_ssq):
        self.p.emit("vector", lambda e: e.memset(ps_sum[:, :], 0.0))
        self.p.emit("vector", lambda e: e.memset(ps_ssq[:, :], 0.0),
                    inc=(self.s_dv, 1))
        return self.s_dv.n

    def stats_mm(self, bank, src, nt, stop):
        def fn(e, bank=bank, src=src, nt=nt, stop=stop):
            return e.matmul(bank[32 * nt:32 * nt + 1, :], self.onec_sb[:, :], src,
                            start=False, stop=stop, skip_group_check=True,
                            tile_position=(0, 32 * nt))
        return fn

    def rows_bcast(self, ps_sum, ps_ssq, pe_done, R, ps_bc, rms=False):
        p, s_pe, s_dv, s_ac, s_st = self.p, self.s_pe, self.s_dv, self.s_ac, self.s_st
        row_t, row_hi, row_lo = R["row_t"], R["row_hi"], R["row_lo"]
        row_ev = R["row_ev"]
        p.emit("vector", lambda e: e.tensor_copy(row_ev[:, 0:512], ps_sum[:, :]),
               wait=((s_pe, pe_done),))
        p.emit("vector", lambda e: e.tensor_copy(row_ev[:, 512:1024], ps_ssq[:, :]),
               inc=(s_dv, 1))
        ev_done = s_dv.n

        if rms:
            def gather(e):
                for j in range(8):
                    src = row_ev[32 * (j % 4):32 * (j % 4) + 1,
                                 (0 if j < 4 else 512):(0 if j < 4 else 512) + 256]
                    e.dma_start(out=R["row_ssq"][0:1, j * 256:(j + 1) * 256],
                                in_=src).then_inc(self.s_st.h, 16)
            ndma = 8
        else:
            def gather(e):
                for nt in range(NT):
                    e.dma_start(out=R["row_sum"][0:1, nt * NW:(nt + 1) * NW],
                                in_=row_ev[32 * nt:32 * nt + 1, 0:512]
                                ).then_inc(self.s_st.h, 16)
                    e.dma_start(out=R["row_ssq"][0:1, nt * NW:(nt + 1) * NW],
                                in_=row_ev[32 * nt:32 * nt + 1, 512:1024]
                                ).then_inc(self.s_st.h, 16)
            ndma = 8
        p.emit("sync", gather, wait=((s_dv, ev_done),))
        s_st.n += 16 * ndma
        g_done = s_st.n

        if rms:
            p.emit("vector", lambda e: e.tensor_scalar(
                row_t[0:1, :], R["row_ssq"][0:1, :], 1.0 / D, RMS_EPS,
                ALU.mult, ALU.add), wait=((s_st, g_done),), inc=(s_dv, 1))
        else:
            p.emit("vector", lambda e: e.tensor_scalar(
                R["row_sum"][0:1, :], R["row_sum"][0:1, :], 1.0 / D, None,
                ALU.mult), wait=((s_st, g_done),))
            p.emit("vector", lambda e: e.tensor_scalar(
                row_t[0:1, :], R["row_ssq"][0:1, :], 1.0 / D, LN_EPS,
                ALU.mult, ALU.add))
            p.emit("vector", lambda e: e.tensor_mul(
                R["row_ssq"][0:1, :], R["row_sum"][0:1, :], R["row_sum"][0:1, :]))
            p.emit("vector", lambda e: e.tensor_sub(
                row_t[0:1, :], row_t[0:1, :], R["row_ssq"][0:1, :]),
                inc=(s_dv, 1))
        t_done = s_dv.n
        p.emit("scalar", lambda e: e.activation(row_t[0:1, :], row_t[0:1, :], AF.Sqrt),
               wait=((s_dv, t_done),), inc=(s_ac, 1))
        p.emit("vector", lambda e: e.reciprocal(row_t[0:1, :], row_t[0:1, :]),
               wait=((s_ac, s_ac.n),))

        bc_free = {}

        def broadcast(src_row, dst, tag):
            p.emit("vector", lambda e: e.tensor_copy(row_hi[0:1, :], src_row[0:1, :]))
            p.emit("vector", lambda e: e.tensor_sub(row_lo[0:1, :], src_row[0:1, :],
                                                    row_hi[0:1, :]),
                   inc=(s_dv, 1))
            rows_done = s_dv.n
            for nt in range(NT):
                p.emit("tensor", lambda e, nt=nt: e.matmul(
                    ps_bc[:, :], self.oner_sb[0:1, :],
                    row_hi[0:1, nt * NW:(nt + 1) * NW], start=True, stop=False),
                    wait=((s_dv, rows_done), (s_dv, bc_free.get("f", 0))))
                p.emit("tensor", lambda e, nt=nt: e.matmul(
                    ps_bc[:, :], self.oner_sb[0:1, :],
                    row_lo[0:1, nt * NW:(nt + 1) * NW], start=False, stop=True),
                    inc=(s_pe, 1))
                mm_done = s_pe.n
                p.emit("vector", lambda e, nt=nt, dst=dst: e.tensor_copy(
                    dst[:, nt * NW:(nt + 1) * NW], ps_bc[:, :]),
                    wait=((s_pe, mm_done),), inc=(s_dv, 1))
                bc_free["f"] = s_dv.n
            return s_dv.n

        done = broadcast(row_t, R["rstd_b"], "r")
        if not rms:
            done = broadcast(R["row_sum"], R["mean_b"], "m")
        return done

    def stats_pass(self, src_img, ps_sum, ps_ssq, xb, sq, extra_wait=()):
        p = self.p
        z = self.zero_stats(ps_sum, ps_ssq)
        pe_after = {0: 0}
        for kc in range(KC):
            p.emit("gpsimd", lambda e, kc=kc: e.dma_start(
                out=xb[:, :], in_=src_img[:, kc * T:(kc + 1) * T]),
                wait=((self.s_pe, pe_after.get(kc, 0)),
                      (self.s_ac, pe_after.get(("a", kc), 0)),
                      (self.s_wl, self.consts_ready)) + tuple(extra_wait),
                inc=(self.s_wl, 16))
            ld = self.s_wl.n
            p.emit("scalar", lambda e: e.activation(sq[:, :], xb[:, :], AF.Square),
                   wait=((self.s_wl, ld), (self.s_pe, pe_after.get(kc, 0))),
                   inc=(self.s_ac, 1))
            sqd = self.s_ac.n
            pe_after[("a", kc + 1)] = sqd
            for nt in range(NT):
                p.emit("tensor", self.stats_mm(
                    ps_sum, xb[:, nt * NW:(nt + 1) * NW], nt, kc == KC - 1),
                    wait=((self.s_wl, ld), (self.s_dv, z)))
                p.emit("tensor", self.stats_mm(
                    ps_ssq, sq[:, nt * NW:(nt + 1) * NW], nt, kc == KC - 1),
                    wait=((self.s_ac, sqd),),
                    inc=(self.s_pe, 1) if nt == NT - 1 else None)
            pe_after[kc + 1] = self.s_pe.n
        return self.s_pe.n

    def normalize(self, src_img, R, bc_done, xf, st_barrier=0):
        p = self.p
        fence = self.fence()
        mul_done = {}
        for kc in range(KC):
            sl = kc % 2
            xf_w = ((self.s_dv, mul_done.get(sl, 0)), (self.s_st, st_barrier))
            if kc < 2:
                xf_w = xf_w + fence
            p.emit("sync", lambda e, kc=kc, sl=sl: e.dma_start(
                out=xf[sl][:, :], in_=src_img[:, kc * T:(kc + 1) * T]),
                wait=xf_w, inc=(self.s_xl, 16))
            ld = self.s_xl.n
            p.emit("gpsimd", lambda e, sl=sl: e.tensor_sub(
                xf[sl][:, :], xf[sl][:, :], R["mean_b"][:, :]),
                wait=((self.s_xl, ld), (self.s_dv, bc_done)),
                inc=(self.s_pl, 1))
            subd = self.s_pl.n
            p.emit("vector", lambda e, kc=kc, sl=sl: e.tensor_mul(
                self.AB(kc), xf[sl][:, :], R["rstd_b"][:, :]),
                wait=((self.s_pl, subd),), inc=(self.s_dv, 1))
            mul_done[sl] = self.s_dv.n
        return self.s_dv.n

    def fmm(self, w_img, n_oc, slabs, ln_ready, evac, slab_cols=KC * 128,
            extra_slab_wait=()):
        p = self.p
        nslab = len(slabs)
        slab_free = {}
        bank_free = {}
        ps_mm = self.ps_mm
        for oc in range(n_oc):
            sl = oc % nslab
            p.emit("gpsimd", lambda e, oc=oc, sl=sl: e.dma_start(
                out=slabs[sl][:, :],
                in_=w_img[:, oc * slab_cols:(oc + 1) * slab_cols]),
                wait=(slab_free.get(sl, (self.s_pe, 0)),) + tuple(extra_slab_wait),
                inc=(self.s_wl, 16))
            ld = self.s_wl.n
            for nt in range(NT):
                bk = (oc * NT + nt) % 2
                for kc in range(KC):
                    wt = [(self.s_wl, ld), (self.s_dv, ln_ready)]
                    if kc == 0:
                        wt.append(bank_free.get(bk, (self.s_pe, 0)))
                    p.emit("tensor", lambda e, sl=sl, kc=kc, nt=nt, bk=bk: e.matmul(
                        ps_mm[bk][:, :],
                        slabs[sl][:, kc * 128:(kc + 1) * 128],
                        self.AB(kc, nt * NW, (nt + 1) * NW),
                        start=(kc == 0), stop=(kc == KC - 1)),
                        wait=tuple(wt),
                        inc=(self.s_pe, 1) if kc == KC - 1 else None)
                mm_done = self.s_pe.n
                bank_free[bk] = evac(oc, nt, ps_mm[bk], mm_done)
            slab_free[sl] = (self.s_pe, self.s_pe.n)
        return

    def ffn(self, x_old_img, x_new_img, with_stats, ps_sum=None, ps_ssq=None,
            tag="f"):
        p = self.p
        s_pe, s_dv, s_ac, s_wl, s_xl, s_st = (self.s_pe, self.s_dv, self.s_ac,
                                              self.s_wl, self.s_xl, self.s_st)
        sc = ExitStack()
        with sc:
            a_slab = self.sb(tag + "a_slab", [128, KC * ER], BF16, sc)
            bw_slab = self.sb(tag + "bw_slab", [128, KC * 128], BF16, sc)
            rt_slab = self.sb(tag + "rt_slab", [128, KC * E], BF16, sc)
            lg_sb = self.sb(tag + "lg_sb", [128, 1024], F32, sc)
            lgt_sb = self.sb(tag + "lgt_sb", [128, 128], F32, sc)
            mx_sb = self.sb(tag + "mx_sb", [128, 16], F32, sc)
            rw_sb = self.sb(tag + "rw_sb", [128, 128], F32, sc)
            rwh_sb = self.sb(tag + "rwh_sb", [128, 128], BF16, sc)
            rwl_sb = self.sb(tag + "rwl_sb", [128, 128], BF16, sc)
            rwt_hi = self.sb(tag + "rwt_hi", [8, T], BF16, sc)
            rwt_lo = self.sb(tag + "rwt_lo", [8, T], BF16, sc)
            h_sb = self.sb(tag + "h_sb", [128, T], BF16, sc)
            wh_sb = self.sb(tag + "wh_sb", [128, T], BF16, sc)
            xo = [self.sb(tag + f"xo{i}", [128, T], F32, sc) for i in range(2)]
            xn = self.sb(tag + "xn", [128, T], F32, sc)
            if with_stats:
                xb = self.sb(tag + "xb", [128, T], BF16, sc)
                sqb = self.sb(tag + "sqb", [128, T], BF16, sc)
            psc = ExitStack()
            with psc:
                ps_lg = [self.psum(tag + f"ps_lg{i}", [128, 512], F32, psc)
                         for i in range(2)]
                ps_loc = self.psum(tag + "ps_loc", [8, 1024], BF16, psc)

                fence = self.fence()
                ln_ready = self.ln_ready

                for dst, src in ((a_slab, self.a_img), (bw_slab, self.bw_img),
                                 (rt_slab, self.rt_img)):
                    p.emit("gpsimd", lambda e, dst=dst, src=src: e.dma_start(
                        out=dst[:, :], in_=src), wait=fence, inc=(s_wl, 16))
                slabs_ld = s_wl.n

                for nt in range(NT):
                    bank = ps_lg[nt // 2]
                    cg = 32 * (nt % 2)
                    for kc in range(KC):
                        p.emit("tensor", lambda e, nt=nt, kc=kc, bank=bank, cg=cg:
                               e.matmul(bank[cg:cg + 8, 0:512],
                                        rt_slab[:, kc * E:(kc + 1) * E],
                                        self.AB(kc, nt * NW, (nt + 1) * NW),
                                        start=(kc == 0), stop=(kc == KC - 1),
                                        tile_position=(0, cg)),
                               wait=((s_wl, slabs_ld), (s_dv, ln_ready)),
                               inc=(s_pe, 1) if kc == KC - 1 else None)
                lg_done = s_pe.n

                bank_free = {}
                for nt in range(NT):
                    bk = nt % 2
                    for kc in range(KC):
                        wt = [(s_wl, slabs_ld), (s_dv, ln_ready)]
                        if kc == 0:
                            wt.append(bank_free.get(bk, (s_pe, 0)))
                        p.emit("tensor", lambda e, nt=nt, kc=kc, bk=bk: e.matmul(
                            self.ps_mm[bk][:, :],
                            a_slab[:, kc * ER:(kc + 1) * ER],
                            self.AB(kc, nt * NW, (nt + 1) * NW),
                            start=(kc == 0), stop=(kc == KC - 1)),
                            wait=tuple(wt),
                            inc=(s_pe, 1) if kc == KC - 1 else None)
                    hmm = s_pe.n
                    p.emit("vector", lambda e, nt=nt, bk=bk: e.tensor_copy(
                        h_sb[:, nt * NW:(nt + 1) * NW], self.ps_mm[bk][:, :]),
                        wait=((s_pe, hmm),), inc=(s_dv, 1))
                    bank_free[bk] = (s_dv, s_dv.n)
                h_done = s_dv.n

                p.emit("vector", lambda e: e.tensor_copy(lg_sb[:, 0:512],
                                                         ps_lg[0][:, :]),
                       wait=((s_pe, lg_done),))
                p.emit("vector", lambda e: e.tensor_copy(lg_sb[:, 512:1024],
                                                         ps_lg[1][:, :]),
                       inc=(s_dv, 1))
                lg_ev = s_dv.n
                for tc in range(16):
                    nt, sub = tc // 4, tc % 4
                    cg = 32 * (nt % 2)
                    src = lg_sb[cg:cg + 8,
                                (nt // 2) * 512 + sub * 128:
                                (nt // 2) * 512 + (sub + 1) * 128]
                    idp = self.id8x_sb[cg:cg + 8, 0:8]
                    p.emit("tensor", lambda e, tc=tc, src=src, idp=idp: e.transpose(
                        self.ps_bc[:, tc * 8:(tc + 1) * 8], src, idp),
                        wait=((s_dv, lg_ev),),
                        inc=(s_pe, 1) if tc == 15 else None)
                tr_done = s_pe.n
                p.emit("vector", lambda e: e.tensor_copy(lgt_sb[:, :],
                                                         self.ps_bc[:, 0:128]),
                       wait=((s_pe, tr_done),), inc=(s_dv, 1))
                bcf = s_dv.n

                for tc in range(16):
                    p.emit("vector", lambda e, tc=tc: e.reduce_max(
                        mx_sb[:, tc:tc + 1], lgt_sb[:, tc * 8:(tc + 1) * 8],
                        axis=AXX))
                for tc in range(16):
                    mxb = bass.AP(mx_sb, tc, [[16, 128], [0, 8]])
                    p.emit("vector", lambda e, tc=tc, mxb=mxb: e.tensor_tensor(
                        rw_sb[:, tc * 8:(tc + 1) * 8],
                        lgt_sb[:, tc * 8:(tc + 1) * 8], mxb, ALU.is_equal))
                for tc in range(16):
                    mxb = bass.AP(mx_sb, tc, [[16, 128], [0, 8]])
                    p.emit("vector", lambda e, tc=tc, mxb=mxb: e.tensor_tensor(
                        rw_sb[:, tc * 8:(tc + 1) * 8],
                        rw_sb[:, tc * 8:(tc + 1) * 8], mxb, ALU.mult))
                p.emit("vector", lambda e: e.tensor_copy(rwh_sb[:, :], rw_sb[:, :]))
                p.emit("vector", lambda e: e.tensor_sub(rwl_sb[:, :], rw_sb[:, :],
                                                        rwh_sb[:, :]),
                       inc=(s_dv, 1))
                rw_done = s_dv.n

                loc_free = 0
                rwt_done = 0
                for half in range(2):
                    for which, (src, dst) in enumerate(((rwh_sb, rwt_hi),
                                                       (rwl_sb, rwt_lo))):
                        for i in range(8):
                            tc = half * 8 + i
                            p.emit("tensor", lambda e, i=i, tc=tc, src=src: e.transpose(
                                ps_loc[0:8, i * 128:(i + 1) * 128],
                                src[:, tc * 8:(tc + 1) * 8], self.id16_sb[:, :]),
                                wait=((s_dv, rw_done), (s_dv, loc_free)),
                                inc=(s_pe, 1) if i == 7 else None)
                        trd = s_pe.n
                        p.emit("vector", lambda e, half=half, dst=dst: e.tensor_copy(
                            dst[0:8, half * 1024:(half + 1) * 1024],
                            ps_loc[0:8, :]),
                            wait=((s_pe, trd),), inc=(s_dv, 1))
                        loc_free = s_dv.n
                rwt_done = s_dv.n

                whd = {}
                for nt in range(NT):
                    p.emit("tensor", lambda e, nt=nt: e.matmul(
                        self.ps_bc[:, :], self.em_sb[0:8, :],
                        rwt_hi[0:8, nt * NW:(nt + 1) * NW], start=True, stop=False),
                        wait=((s_dv, rwt_done), (s_dv, bcf),
                              (s_dv, whd.get("bc", 0))))
                    p.emit("tensor", lambda e, nt=nt: e.matmul(
                        self.ps_bc[:, :], self.em_sb[0:8, :],
                        rwt_lo[0:8, nt * NW:(nt + 1) * NW], start=False, stop=True),
                        inc=(s_pe, 1))
                    rwe = s_pe.n
                    p.emit("vector", lambda e, nt=nt: e.tensor_mul(
                        wh_sb[:, nt * NW:(nt + 1) * NW],
                        h_sb[:, nt * NW:(nt + 1) * NW], self.ps_bc[:, :]),
                        wait=((s_pe, rwe), (s_dv, h_done)), inc=(s_dv, 1))
                    whd["bc"] = s_dv.n
                wh_done = s_dv.n

                if self.tap == "ffn1" and tag == "f1":
                    w = ((s_pe, s_pe.n), (s_dv, s_dv.n), (s_ac, s_ac.n))
                    for dst_lo, src in ((0, h_sb[:, :]), (2048, wh_sb[:, :]),
                                        (6144, rwh_sb[:, :]), (6272, rwl_sb[:, :])):
                        p.emit("gpsimd", lambda e, dst_lo=dst_lo, src=src:
                               e.dma_start(out=self.tapb_img[
                                   0:src.shape[0], dst_lo:dst_lo + src.shape[1]],
                                   in_=src), wait=w, inc=(s_st, 16))
                    p.emit("gpsimd", lambda e: e.dma_start(
                        out=self.tapb_img[0:8, 8192:8192 + 2048],
                        in_=rwt_hi[0:8, :]), wait=w, inc=(s_st, 16))
                    p.emit("gpsimd", lambda e: e.dma_start(
                        out=self.tapb_img[0:8, 10240:10240 + 2048],
                        in_=rwt_lo[0:8, :]), wait=w, inc=(s_st, 16))
                    for dst_lo, src in ((0, lg_sb[:, :]), (1024, lgt_sb[:, :]),
                                        (1152, rw_sb[:, :]), (1280, mx_sb[:, :])):
                        p.emit("gpsimd", lambda e, dst_lo=dst_lo, src=src:
                               e.dma_start(out=self.tap_img[
                                   0:128, dst_lo:dst_lo + src.shape[1]],
                                   in_=src), wait=w, inc=(s_st, 16))

                if with_stats:
                    z = self.zero_stats(ps_sum, ps_ssq)
                xo_free = {}
                xn_free = (s_st, 0)
                cast_free = (s_pe, 0)
                pe_stats = 0
                for oc in range(KC):
                    sl = oc % 2
                    xo_w = (xo_free.get(sl, (s_dv, 0)),)
                    if oc < 2:
                        xo_w = xo_w + fence
                    p.emit("sync", lambda e, oc=oc, sl=sl: e.dma_start(
                        out=xo[sl][:, :],
                        in_=x_old_img[:, oc * T:(oc + 1) * T]),
                        wait=xo_w, inc=(s_xl, 16))
                    xold_ld = s_xl.n
                    dv_nt = 0
                    for nt in range(NT):
                        bk = nt % 2
                        wt = [(s_dv, wh_done)]
                        if True:
                            wt.append(self.ffn_bank_free.get(bk, (s_dv, 0)))
                        p.emit("tensor", lambda e, oc=oc, nt=nt, bk=bk: e.matmul(
                            self.ps_mm[bk][:, :],
                            bw_slab[:, oc * 128:(oc + 1) * 128],
                            wh_sb[:, nt * NW:(nt + 1) * NW],
                            start=True, stop=True),
                            wait=tuple(wt), inc=(s_pe, 1))
                        bmm = s_pe.n
                        p.emit("vector", lambda e, nt=nt, bk=bk, sl=sl:
                               e.scalar_tensor_tensor(
                                   xn[:, nt * NW:(nt + 1) * NW],
                                   self.ps_mm[bk][:, :], 0.5,
                                   xo[sl][:, nt * NW:(nt + 1) * NW],
                                   ALU.mult, ALU.add),
                               wait=((s_pe, bmm), (s_xl, xold_ld),
                                     (s_st, xn_free[1]), (s_ac, cast_free[1])),
                               inc=(s_dv, 1))
                        self.ffn_bank_free[bk] = (s_dv, s_dv.n)
                        dv_nt = s_dv.n
                    xo_free[sl] = (s_dv, dv_nt)
                    p.emit("sync", lambda e, oc=oc: e.dma_start(
                        out=x_new_img[:, oc * T:(oc + 1) * T], in_=xn[:, :]),
                        wait=((s_dv, dv_nt),), inc=(s_st, 16))
                    xn_free = (s_st, s_st.n)
                    if with_stats:
                        p.emit("scalar", lambda e: e.activation(
                            xb[:, :], xn[:, :], AF.Copy),
                            wait=((s_dv, dv_nt), (s_pe, pe_stats)))
                        p.emit("scalar", lambda e: e.activation(
                            sqb[:, :], xn[:, :], AF.Square), inc=(s_ac, 1))
                        cast_done = s_ac.n
                        cast_free = (s_ac, cast_done)
                        for nt in range(NT):
                            p.emit("tensor", self.stats_mm(
                                ps_sum, xb[:, nt * NW:(nt + 1) * NW], nt,
                                oc == KC - 1),
                                wait=((s_ac, cast_done), (s_dv, z)))
                            p.emit("tensor", self.stats_mm(
                                ps_ssq, sqb[:, nt * NW:(nt + 1) * NW], nt,
                                oc == KC - 1),
                                inc=(s_pe, 1) if nt == NT - 1 else None)
                        pe_stats = s_pe.n
                return (pe_stats if with_stats else s_pe.n), s_st.n

    def qkv_prod(self):
        p = self.p
        s_pe, s_dv, s_ac, s_wl, s_st = (self.s_pe, self.s_dv, self.s_ac,
                                        self.s_wl, self.s_st)
        ln_ready = self.ln_ready
        fence = self.fence()
        scv = ExitStack()
        with scv:
            vsl = [self.sb(f"vsl{i}", [128, KC * 512], BF16, scv) for i in range(2)]
            vst = [self.sb(f"vst{i}", [128, 512], BF16, scv) for i in range(2)]
            slab_free = {}
            stage_free = {}
            bank_free = {}
            for g in range(8):
                sl = g % 2
                p.emit("gpsimd", lambda e, g=g, sl=sl: e.dma_start(
                    out=vsl[sl][:, :],
                    in_=self.wv_img[:, g * KC * 512:(g + 1) * KC * 512]),
                    wait=(slab_free.get(sl, (s_pe, 0)),) + fence,
                    inc=(s_wl, 16))
                ld = self.s_wl.n
                for tc in range(16):
                    bk = tc % 2
                    for kc in range(KC):
                        wt = [(s_wl, ld), (s_dv, ln_ready)]
                        if kc == 0:
                            wt.append(bank_free.get(bk, (s_ac, 0)))
                        p.emit("tensor", lambda e, sl=sl, kc=kc, tc=tc, bk=bk:
                               e.matmul(self.ps_mm[bk][:, :],
                                        self.AB(kc, tc * 128, (tc + 1) * 128),
                                        vsl[sl][:, kc * 512:(kc + 1) * 512],
                                        start=(kc == 0), stop=(kc == KC - 1)),
                               wait=tuple(wt),
                               inc=(s_pe, 1) if kc == KC - 1 else None)
                    mmd = s_pe.n
                    st_sl = tc % 2
                    p.emit("scalar", lambda e, st_sl=st_sl, bk=bk: e.activation(
                        vst[st_sl][:, :], self.ps_mm[bk][:, :], AF.Silu),
                        wait=((s_pe, mmd), (s_st, stage_free.get(st_sl, 0))),
                        inc=(s_ac, 1))
                    bank_free[bk] = (s_ac, s_ac.n)
                    dst = self.v_img.rearrange("p (h t) -> p h t", h=H)[
                        :, 4 * g:4 * g + 4, tc * 128:(tc + 1) * 128]
                    src = vst[st_sl][:, :].rearrange("p (j d) -> p j d", j=4)
                    p.emit("scalar", lambda e, dst=dst, src=src: e.dma_start(
                        out=dst, in_=src), inc=(s_st, 16))
                    stage_free[st_sl] = s_st.n
                slab_free[sl] = (s_pe, s_pe.n)

        scq = ExitStack()
        with scq:
            qsl = [self.sb(f"qsl{i}", [128, KC * 128], BF16, scq) for i in range(3)]
            qst = [self.sb(f"qst{i}", [128, T], BF16, scq) for i in range(2)]
            fence_q = self.fence()
            slab_free = {}
            stage_free = {}
            bank_free = {}
            idx = 0
            for h in range(H):
                for w_img, dst_img in ((self.wq_img, self.qt_img),
                                       (self.wk_img, self.kt_img)):
                    sl = idx % 3
                    q_w = (slab_free.get(sl, (s_pe, 0)),)
                    if idx < 3:
                        q_w = q_w + fence_q
                    p.emit("gpsimd", lambda e, h=h, sl=sl, w_img=w_img: e.dma_start(
                        out=qsl[sl][:, :],
                        in_=w_img[:, h * KC * 128:(h + 1) * KC * 128]),
                        wait=q_w, inc=(s_wl, 16))
                    ld = self.s_wl.n
                    st_sl = idx % 2
                    for nt in range(NT):
                        bk = nt % 2
                        for kc in range(KC):
                            wt = [(s_wl, ld), (s_dv, ln_ready)]
                            if kc == 0:
                                wt.append(bank_free.get(bk, (s_ac, 0)))
                                if nt == 0:
                                    wt.append((s_st, stage_free.get(st_sl, 0)))
                            p.emit("tensor", lambda e, sl=sl, kc=kc, nt=nt, bk=bk:
                                   e.matmul(self.ps_mm[bk][:, :],
                                            qsl[sl][:, kc * 128:(kc + 1) * 128],
                                            self.AB(kc, nt * NW, (nt + 1) * NW),
                                            start=(kc == 0), stop=(kc == KC - 1)),
                                   wait=tuple(wt),
                                   inc=(s_pe, 1) if kc == KC - 1 else None)
                        mmd = s_pe.n
                        p.emit("scalar", lambda e, st_sl=st_sl, nt=nt, bk=bk:
                               e.activation(qst[st_sl][:, nt * NW:(nt + 1) * NW],
                                            self.ps_mm[bk][:, :], AF.Silu),
                               wait=((s_pe, mmd),), inc=(s_ac, 1))
                        bank_free[bk] = (s_ac, s_ac.n)
                    p.emit("scalar", lambda e, h=h, st_sl=st_sl, dst_img=dst_img:
                           e.dma_start(out=dst_img[:, h * T:(h + 1) * T],
                                       in_=qst[st_sl][:, :]),
                           inc=(s_st, 16))
                    stage_free[st_sl] = s_st.n
                    slab_free[sl] = (s_pe, s_pe.n)
                    idx += 1
        return s_st.n

    def attn_core(self):
        p = self.p
        s_pe, s_dv, s_ac, s_xl, s_st = (self.s_pe, self.s_dv, self.s_ac,
                                        self.s_xl, self.s_st)
        sc = ExitStack()
        with sc:
            qt = [self.sb(f"ac_qt{i}", [128, T], BF16, sc) for i in range(2)]
            kt = [self.sb(f"ac_kt{i}", [128, T], BF16, sc) for i in range(2)]
            vb = [self.sb(f"ac_vb{i}", [128, T], BF16, sc) for i in range(2)]
            qs = [self.sb(f"ac_qs{i}", [128, T], BF16, sc) for i in range(2)]
            mk = [self.sb(f"ac_mk{i}", [128, 512], F32, sc) for i in range(2)]
            qd = [self.sb(f"ac_qd{i}", [128, 256], F32, sc) for i in range(2)]
            scb = [self.sb(f"ac_sc{i}", [128, 512], BF16, sc) for i in range(2)]
            ksb = [self.sb(f"ac_ks{i}", [128, 256], BF16, sc) for i in range(2)]
            kv32 = [self.sb(f"ac_kv{i}", [128, 128], F32, sc) for i in range(2)]
            kvbf = [self.sb(f"ac_kb{i}", [128, 128], BF16, sc) for i in range(2)]
            ats = [self.sb(f"ac_at{i}", [128, T], BF16, sc) for i in range(2)]
            psc = ExitStack()
            with psc:
                ps_s = [self.psum(f"ac_ps_s{i}", [128, 512], F32, psc)
                        for i in range(2)]
                ps_kt = [self.psum(f"ac_ps_kt{i}", [128, 256], BF16, psc)
                         for i in range(2)]
                ps_a = [self.psum(f"ac_ps_a{i}", [128, 256], F32, psc)
                        for i in range(2)]
                ps_kv = [self.psum(f"ac_ps_kv{i}", [128, 128], F32, psc)
                         for i in range(2)]

                fence = self.fence()
                loads = {}
                head_pe = {}
                head_dv = {}
                head_ac = {}
                head_st = {}
                mask_dv = {}
                ks_ac = {}
                evac_ac = {}
                kv_dv = {}
                kvbf_dv = {}
                qs_dv = {}

                def emit_loads(h):
                    sl = h % 2
                    w = [(s_pe, head_pe.get(h - 2, 0)),
                         (s_dv, head_dv.get(h - 2, 0)),
                         (s_ac, head_ac.get(h - 2, 0))]
                    if h < 2:
                        w += list(fence)
                    for dst, src in ((qt[sl], self.qt_img), (kt[sl], self.kt_img),
                                     (vb[sl], self.v_img)):
                        p.emit("sync", lambda e, dst=dst, src=src, h=h: e.dma_start(
                            out=dst[:, :], in_=src[:, h * T:(h + 1) * T]),
                            wait=tuple(w), inc=(s_xl, 16))
                    p.emit("sync", lambda e, sl=sl, h=h: e.dma_start(
                        out=mk[sl][:, :], in_=self.mk_img[:, h * 512:(h + 1) * 512]),
                        wait=tuple(w), inc=(s_xl, 16))
                    p.emit("sync", lambda e, sl=sl, h=h: e.dma_start(
                        out=qd[sl][:, :], in_=self.qd_img[:, h * 256:(h + 1) * 256]),
                        wait=tuple(w), inc=(s_xl, 16))
                    loads[h] = s_xl.n

                emit_loads(0)
                emit_loads(1)

                bcnt = 0

                for h in range(H):
                    sl = h % 2
                    bd = float(BD_TAB[h])
                    ld = loads[h]
                    qd_rep = bass.AP(qd[sl], 0, [[256, 128], [0, NB], [1, 256]])
                    qt_v = bass.AP(qt[sl], 0, [[T, 128], [256, NB], [1, 256]])
                    qs_v = bass.AP(qs[sl], 0, [[T, 128], [256, NB], [1, 256]])
                    p.emit("vector", lambda e, qs_v=qs_v, qt_v=qt_v, qd_rep=qd_rep:
                           e.tensor_tensor(qs_v, qt_v, qd_rep, ALU.mult),
                           wait=((s_xl, ld), (s_pe, qs_dv.get(h - 2, 0))),
                           inc=(s_dv, 1))
                    qs_done = s_dv.n
                    p.emit("vector", lambda e, sl=sl: e.memset(kv32[0][:, :], 0.0),
                           wait=((s_st, head_st.get(h - 1, 0)),), inc=(s_dv, 1))
                    kv_dv[(h, -1)] = s_dv.n

                    def S(j, b):
                        for mc in range(2):
                            wt = [(s_xl, ld)]
                            if mc == 0:
                                wt.append((s_dv, mask_dv.get(b - 2, 0)))
                            p.emit("tensor", lambda e, j=j, mc=mc, b=b, sl=sl:
                                   e.matmul(ps_s[b % 2][:, mc * 256:(mc + 1) * 256],
                                            kt[sl][:, j * 256 + mc * 128:
                                                   j * 256 + (mc + 1) * 128],
                                            qt[sl][:, j * 256:(j + 1) * 256],
                                            start=True, stop=True),
                                   wait=tuple(wt),
                                   inc=(s_pe, 1) if mc == 1 else None)
                        return s_pe.n

                    def TR(j, b):
                        for mc in range(2):
                            wt = []
                            if mc == 0:
                                wt.append((s_ac, ks_ac.get(b - 2, 0)))
                            p.emit("tensor", lambda e, j=j, mc=mc, b=b, sl=sl:
                                   e.transpose(
                                       ps_kt[b % 2][:, mc * 128:(mc + 1) * 128],
                                       kt[sl][:, j * 256 + mc * 128:
                                              j * 256 + (mc + 1) * 128],
                                       self.id16_sb[:, :]),
                                   wait=tuple(wt),
                                   inc=(s_pe, 1) if mc == 1 else None)
                        return s_pe.n

                    def MASK(j, b, s_done):
                        p.emit("vector", lambda e, j=j, b=b, sl=sl: e.tensor_mul(
                            scb[b % 2][:, :], ps_s[b % 2][:, :], mk[sl][:, :]),
                            wait=((s_pe, s_done),), inc=(s_dv, 1))
                        mask_dv[b] = s_dv.n
                        return s_dv.n

                    def KS(j, b, t_done):
                        for mc in range(2):
                            p.emit("scalar", lambda e, j=j, mc=mc, b=b, h=h:
                                   e.activation(
                                       ksb[b % 2][:, mc * 128:(mc + 1) * 128],
                                       ps_kt[b % 2][:, mc * 128:(mc + 1) * 128],
                                       AF.Copy,
                                       scale=self.kd_sb[:, h * 2 + mc:h * 2 + mc + 1]),
                                   wait=((s_pe, t_done),) if mc == 0 else (),
                                   inc=(s_ac, 1) if mc == 1 else None)
                        ks_ac[b] = s_ac.n
                        return s_ac.n

                    def IXU(j, b, m_done, k_done):
                        for mc in range(2):
                            wt = [(s_dv, m_done), (s_xl, ld)]
                            if mc == 0:
                                wt.append((s_ac, evac_ac.get(b - 2, 0)))
                            p.emit("tensor", lambda e, j=j, mc=mc, b=b, sl=sl:
                                   e.matmul(ps_a[b % 2][:, :],
                                            vb[sl][:, (2 * j + mc) * 128:
                                                   (2 * j + mc + 1) * 128],
                                            scb[b % 2][:, mc * 256:(mc + 1) * 256],
                                            start=(mc == 0),
                                            stop=(mc == 1 and j == 0)),
                                   wait=tuple(wt))
                        if j > 0:
                            p.emit("tensor", lambda e, j=j, b=b, sl=sl: e.matmul(
                                ps_a[b % 2][:, :], kvbf[j % 2][:, :],
                                qs[sl][:, j * 256:(j + 1) * 256],
                                start=False, stop=True),
                                wait=((s_dv, kvbf_dv[(h, j - 1)]), (s_dv, qs_done)))
                        for mc in range(2):
                            wt = [(s_ac, k_done)]
                            if mc == 0:
                                wt.append((s_dv, kv_dv.get((h, j - 2), 0)))
                            p.emit("tensor", lambda e, j=j, mc=mc, b=b, sl=sl:
                                   e.matmul(ps_kv[j % 2][:, :],
                                            ksb[b % 2][:, mc * 128:(mc + 1) * 128],
                                            vb[sl][:, (2 * j + mc) * 128:
                                                   (2 * j + mc + 1) * 128],
                                            start=(mc == 0), stop=(mc == 1)),
                                   wait=tuple(wt),
                                   inc=(s_pe, 1) if mc == 1 else None)
                        return s_pe.n

                    def EPI(j, b, ixu_done):
                        p.emit("scalar", lambda e, j=j, b=b, sl=sl: e.activation(
                            ats[sl][:, j * 256:(j + 1) * 256], ps_a[b % 2][:, :],
                            AF.Copy),
                            wait=((s_pe, ixu_done), (s_st, head_st.get(h - 2, 0))),
                            inc=(s_ac, 1))
                        evac_ac[b] = s_ac.n
                        p.emit("vector", lambda e, j=j, bd=bd: e.scalar_tensor_tensor(
                            kv32[(j + 1) % 2][:, :], kv32[j % 2][:, :], bd,
                            ps_kv[j % 2][:, :], ALU.mult, ALU.add),
                            wait=((s_pe, ixu_done),), inc=(s_dv, 1))
                        kv_dv[(h, j)] = s_dv.n
                        if j < NB - 1:
                            p.emit("vector", lambda e, j=j: e.tensor_copy(
                                kvbf[(j + 1) % 2][:, :], kv32[(j + 1) % 2][:, :]),
                                inc=(s_dv, 1))
                            kvbf_dv[(h, j)] = s_dv.n

                    LOOKAHEAD = False
                    if LOOKAHEAD:
                        pend = None
                        for j in range(NB):
                            b = bcnt
                            bcnt += 1
                            sd = S(j, b)
                            td = TR(j, b)
                            md = MASK(j, b, sd)
                            kd_ = KS(j, b, td)
                            if pend is not None:
                                ixu = IXU(*pend)
                                EPI(pend[0], pend[1], ixu)
                            pend = (j, b, md, kd_)
                        ixu = IXU(*pend)
                        EPI(pend[0], pend[1], ixu)
                    else:
                        for j in range(NB):
                            b = bcnt
                            bcnt += 1
                            sd = S(j, b)
                            td = TR(j, b)
                            md = MASK(j, b, sd)
                            kd_ = KS(j, b, td)
                            ixu = IXU(j, b, md, kd_)
                            EPI(j, b, ixu)

                    head_pe[h] = s_pe.n
                    head_dv[h] = s_dv.n
                    head_ac[h] = s_ac.n
                    qs_dv[h] = s_pe.n
                    p.emit("scalar", lambda e, h=h, sl=sl: e.dma_start(
                        out=self.at_img[:, h * T:(h + 1) * T], in_=ats[sl][:, :]),
                        inc=(s_st, 16))
                    p.emit("gpsimd", lambda e, h=h: e.dma_start(
                        out=self.u_in[h * HD:(h + 1) * HD, :], in_=kv32[0][:, :]),
                        wait=((s_dv, kv_dv[(h, NB - 1)]),), inc=(s_st, 16))
                    head_st[h] = s_st.n
                    if h + 2 < H:
                        emit_loads(h + 2)
        return s_st.n

    def collective(self, u_stores_done):
        p = self.p
        p.emit("gpsimd", lambda e: e.collective_compute(
            "AllGather", ALU.bypass, replica_groups=[list(range(NCORE))],
            ins=[self.u_in], outs=[self.u_out]),
            wait=((self.s_st, u_stores_done),), inc=(self.s_cc, 1))
        return self.s_cc.n

    def gate(self):
        p = self.p
        s_pe, s_ac, s_st, s_wl = self.s_pe, self.s_ac, self.s_st, self.s_wl
        sc = ExitStack()
        with sc:
            gsl = [self.sb(f"g_sl{i}", [128, KC * 128], BF16, sc) for i in range(2)]
            gst = [self.sb(f"g_st{i}", [128, T], BF16, sc) for i in range(2)]
            stage_free = {}

            def evac(oc, nt, bank, mm_done):
                sl = oc % 2
                wt = [(s_pe, mm_done)]
                if nt == 0:
                    wt.append((s_st, stage_free.get(sl, 0)))
                p.emit("scalar", lambda e, sl=sl, nt=nt, bank=bank: e.activation(
                    gst[sl][:, nt * NW:(nt + 1) * NW], bank[:, :], AF.Sigmoid),
                    wait=tuple(wt), inc=(s_ac, 1))
                if nt == NT - 1:
                    p.emit("scalar", lambda e, oc=oc, sl=sl: e.dma_start(
                        out=self.g_img[:, oc * T:(oc + 1) * T], in_=gst[sl][:, :]),
                        inc=(s_st, 16))
                    stage_free[sl] = s_st.n
                return (s_ac, s_ac.n)

            self.fmm(self.wg_img, KC, gsl, self.ln_ready, evac,
                     extra_slab_wait=self.fence())
        return s_st.n

    def assemble(self, cc_done, ps_ss, ps_corr):
        p = self.p
        s_pe, s_dv, s_ac, s_xl, s_st = (self.s_pe, self.s_dv, self.s_ac,
                                        self.s_xl, self.s_st)
        sc = ExitStack()
        with sc:
            atc = [self.sb(f"p5_at{i}", [128, T], BF16, sc) for i in range(2)]
            gc = [self.sb(f"p5_g{i}", [128, T], BF16, sc) for i in range(2)]
            qtc = [self.sb(f"p5_qt{i}", [128, T], BF16, sc) for i in range(2)]
            qdc = [self.sb(f"p5_qd{i}", [128, 256], F32, sc) for i in range(2)]
            qsc1 = self.sb("p5_qs", [128, T], BF16, sc)
            qsc = [qsc1, qsc1]
            ut = [self.sb(f"p5_ut{i}", [128, NCORE * 128], F32, sc) for i in range(2)]
            gnw1 = self.sb("p5_gnw", [128, T], BF16, sc)
            gnw = [gnw1, gnw1]
            kva = self.sb("p5_kva", [128, 128], F32, sc)
            kvb = self.sb("p5_kvb", [128, 128], F32, sc)
            kbf = [self.sb(f"p5_kbf{i}", [128, 128], BF16, sc) for i in range(2)]
            af = [self.sb(f"p5_af{i}", [128, 256], BF16, sc) for i in range(2)]
            sqb = [self.sb(f"p5_sq{i}", [128, 256], BF16, sc) for i in range(2)]

            fence = self.fence()
            z = self.zero_stats(ps_ss[0], ps_ss[1])
            loads = {}
            head_done = {}
            head_acd = {}
            head_ped = {}

            def emit_loads(h):
                sl = h % 2
                w = [(s_dv, head_done.get(h - 2, 0)),
                     (s_ac, head_acd.get(h - 2, 0)),
                     (s_pe, head_ped.get(h - 2, 0)),
                     (self.s_cc, cc_done)]
                if h < 2:
                    w += list(fence)
                for dst, src in ((atc[sl], self.at_img), (qtc[sl], self.qt_img)):
                    p.emit("sync", lambda e, dst=dst, src=src, h=h: e.dma_start(
                        out=dst[:, :], in_=src[:, h * T:(h + 1) * T]),
                        wait=tuple(w), inc=(s_xl, 16))
                p.emit("sync", lambda e, sl=sl, h=h: e.dma_start(
                    out=gc[sl][:, :], in_=self.g_img[:, h * T:(h + 1) * T]),
                    wait=tuple(w), inc=(s_xl, 16))
                p.emit("sync", lambda e, sl=sl, h=h: e.dma_start(
                    out=qdc[sl][:, :], in_=self.qd_img[:, h * 256:(h + 1) * 256]),
                    wait=tuple(w), inc=(s_xl, 16))
                for r in range(NCORE):
                    p.emit("sync", lambda e, sl=sl, h=h, r=r: e.dma_start(
                        out=ut[sl][:, r * 128:(r + 1) * 128],
                        in_=self.u_out[(r * H + h) * HD:(r * H + h + 1) * HD, :]),
                        wait=tuple(w), inc=(s_xl, 16))
                loads[h] = s_xl.n

            emit_loads(0)
            emit_loads(1)

            corr_free = {}
            af_free = {}
            for h in range(H):
                sl = h % 2
                bd = float(BD_TAB[h])
                ld = loads[h]
                p.emit("scalar", lambda e, sl=sl, h=h: e.activation(
                    gnw[sl][:, :], gc[sl][:, :], AF.Copy,
                    scale=self.nw_sb[:, h:h + 1]),
                    wait=((s_xl, ld), (s_dv, head_done.get(h - 1, 0))),
                    inc=(s_ac, 1))
                gnw_done = s_ac.n
                qd_rep = bass.AP(qdc[sl], 0, [[256, 128], [0, NB], [1, 256]])
                qt_v = bass.AP(qtc[sl], 0, [[T, 128], [256, NB], [1, 256]])
                qs_v = bass.AP(qsc[sl], 0, [[T, 128], [256, NB], [1, 256]])
                p.emit("vector", lambda e, qs_v=qs_v, qt_v=qt_v, qd_rep=qd_rep:
                       e.tensor_tensor(qs_v, qt_v, qd_rep, ALU.mult),
                       wait=((s_xl, ld), (s_pe, head_ped.get(h - 1, 0))),
                       inc=(s_dv, 1))
                qs_done = s_dv.n
                cur, nxt = kva, kvb
                p.emit("vector", lambda e, sl=sl, h=h, cur=cur: e.tensor_scalar(
                    cur[:, :], ut[sl][:, 0:128],
                    self.cc_sb[:, h:h + 1], None, ALU.mult),
                    wait=((s_xl, ld),))
                for r in range(1, NCORE):
                    p.emit("vector", lambda e, sl=sl, h=h, r=r, cur=cur:
                           e.scalar_tensor_tensor(
                               cur[:, :], ut[sl][:, r * 128:(r + 1) * 128],
                               self.cc_sb[:, r * H + h:r * H + h + 1],
                               cur[:, :], ALU.mult, ALU.add))
                p.emit("vector", lambda e, cur=cur: e.tensor_copy(
                    kbf[0][:, :], cur[:, :]),
                    wait=((s_pe, head_ped.get(h - 1, 0)),), inc=(s_dv, 1))
                kbf_done = {0: s_dv.n}
                corr_mms = {}

                for j in range(NB):
                    p.emit("tensor", lambda e, j=j, sl=sl: e.matmul(
                        ps_corr[j % 2][:, :], kbf[j % 2][:, :],
                        qsc[sl][:, j * 256:(j + 1) * 256],
                        start=True, stop=True),
                        wait=((s_dv, kbf_done[j]), (s_dv, qs_done),
                              (s_dv, corr_free.get(j % 2, 0))),
                        inc=(s_pe, 1))
                    corr_mm = s_pe.n
                    corr_mms[j] = corr_mm
                    if j < NB - 1:
                        p.emit("vector", lambda e, cur=cur, nxt=nxt, bd=bd:
                               e.tensor_scalar(
                                   nxt[:, :], cur[:, :], bd, None, ALU.mult))
                        p.emit("vector", lambda e, j=j, nxt=nxt: e.tensor_copy(
                            kbf[(j + 1) % 2][:, :], nxt[:, :]),
                            wait=((s_pe, corr_mms.get(j - 1, 0)),),
                            inc=(s_dv, 1))
                        kbf_done[j + 1] = s_dv.n
                        cur, nxt = nxt, cur
                    p.emit("vector", lambda e, j=j, sl=sl: e.tensor_add(
                        af[j % 2][:, :], ps_corr[j % 2][:, :],
                        atc[sl][:, j * 256:(j + 1) * 256]),
                        wait=((s_pe, corr_mm), (s_ac, af_free.get(j % 2, 0)),
                              (s_pe, af_free.get(("p", j % 2), 0))),
                        inc=(s_dv, 1))
                    af_done = s_dv.n
                    corr_free[j % 2] = af_done
                    p.emit("vector", lambda e, j=j, sl=sl, h=h: e.tensor_mul(
                        self.act_buf[:, h * T + j * 256: h * T + (j + 1) * 256],
                        af[j % 2][:, :], gnw[sl][:, j * 256:(j + 1) * 256]),
                        wait=((s_ac, gnw_done),), inc=(s_dv, 1))
                    p.emit("scalar", lambda e, j=j: e.activation(
                        sqb[j % 2][:, :], af[j % 2][:, :], AF.Square),
                        wait=((s_dv, af_done),
                              (s_pe, af_free.get(("p", j % 2), 0))),
                        inc=(s_ac, 1))
                    sq_done = s_ac.n
                    af_free[j % 2] = sq_done
                    p.emit("tensor", lambda e, j=j: e.matmul(
                        ps_ss[j // 4][32 * (j % 4):32 * (j % 4) + 1, 0:256],
                        self.onec_sb[:, :], sqb[j % 2][:, :],
                        start=False, stop=(h == H - 1), skip_group_check=True,
                        tile_position=(0, 32 * (j % 4))),
                        wait=((s_ac, sq_done), (s_dv, z)), inc=(self.s_pe, 1))
                    af_free[("p", j % 2)] = self.s_pe.n
                head_done[h] = s_dv.n
                head_acd[h] = s_ac.n
                head_ped[h] = s_pe.n
                if h + 2 < H:
                    emit_loads(h + 2)
            ss_done = self.s_pe.n
        return ss_done

    def zproj(self, rb, bc_done, ps_sum, ps_ssq, with_stats=True):
        p = self.p
        s_pe, s_dv, s_ac, s_xl, s_st = (self.s_pe, self.s_dv, self.s_ac,
                                        self.s_xl, self.s_st)
        sc = ExitStack()
        with sc:
            osl = [self.sb(f"z_sl{i}", [128, KC * 128], BF16, sc) for i in range(2)]
            xo = [self.sb(f"z_xo{i}", [128, T], F32, sc) for i in range(2)]
            xn = self.sb("z_xn", [128, T], F32, sc)
            tsb = self.sb("z_t", [128, NW], F32, sc)
            xb = self.sb("z_xb", [128, T], BF16, sc)
            sqb = self.sb("z_sq", [128, T], BF16, sc)

            z = self.zero_stats(ps_sum, ps_ssq)
            zfence = self.fence()
            xo_free = {}
            xn_free = {"st": 0}
            t_free = {}
            cast_free = {}
            pe_stats = {"n": 0}
            state = {"dv_nt": 0}

            def evac(oc, nt, bank, mm_done):
                sl = oc % 2
                if nt == 0:
                    xo_w = (xo_free.get(sl, (s_dv, 0)),)
                    if oc < 2:
                        xo_w = xo_w + zfence
                    p.emit("sync", lambda e, oc=oc, sl=sl: e.dma_start(
                        out=xo[sl][:, :],
                        in_=self.x1_img[:, oc * T:(oc + 1) * T]),
                        wait=xo_w, inc=(s_xl, 16))
                    xo_free["ld", sl] = s_xl.n
                p.emit("vector", lambda e, nt=nt, bank=bank: e.tensor_mul(
                    tsb[:, :], bank[:, :], rb["rstd_b"][:, nt * NW:(nt + 1) * NW]),
                    wait=((s_pe, mm_done), (s_dv, bc_done)))
                p.emit("vector", lambda e, nt=nt, sl=sl: e.tensor_add(
                    xn[:, nt * NW:(nt + 1) * NW], tsb[:, :],
                    xo[sl][:, nt * NW:(nt + 1) * NW]),
                    wait=((s_xl, xo_free.get(("ld", sl), 0)),
                          (s_st, xn_free["st"]), (s_ac, cast_free.get(0, 0))),
                    inc=(s_dv, 1))
                state["dv_nt"] = s_dv.n
                if nt == NT - 1:
                    dvd = s_dv.n
                    xo_free[sl] = (s_dv, dvd)
                    p.emit("sync", lambda e, oc=oc: e.dma_start(
                        out=self.x2_img[:, oc * T:(oc + 1) * T], in_=xn[:, :]),
                        wait=((s_dv, dvd),), inc=(s_st, 16))
                    xn_free["st"] = s_st.n
                    if with_stats:
                        p.emit("scalar", lambda e: e.activation(
                            xb[:, :], xn[:, :], AF.Copy),
                            wait=((s_dv, dvd), (s_pe, pe_stats["n"])))
                        p.emit("scalar", lambda e: e.activation(
                            sqb[:, :], xn[:, :], AF.Square), inc=(s_ac, 1))
                        cd = s_ac.n
                        cast_free[0] = cd
                        for nt2 in range(NT):
                            p.emit("tensor", self.stats_mm(
                                ps_sum, xb[:, nt2 * NW:(nt2 + 1) * NW], nt2,
                                oc == KC - 1),
                                wait=((s_ac, cd), (s_dv, z)))
                            p.emit("tensor", self.stats_mm(
                                ps_ssq, sqb[:, nt2 * NW:(nt2 + 1) * NW], nt2,
                                oc == KC - 1),
                                inc=(s_pe, 1) if nt2 == NT - 1 else None)
                        pe_stats["n"] = s_pe.n
                return (s_dv, state["dv_nt"])

            self.fmm(self.wo_img, KC, osl, self.y_ready, evac,
                     extra_slab_wait=self.fence())
        return pe_stats["n"], s_st.n

    def build(self):
        p = self.p
        self.alloc_globals()
        self.ffn_bank_free = {}

        scA = ExitStack()
        with scA:
            self.ps_mm = [self.psum(f"ps_mm{i}", [128, 512], F32, scA)
                          for i in range(2)]
            self.ps_bc = self.psum("ps_bc", [128, 512], F32, scA)
            ps_sum = self.psum("ps_sum", [128, 512], F32, scA)
            ps_ssq = self.psum("ps_ssq", [128, 512], F32, scA)

            sc0 = ExitStack()
            with sc0:
                R = self.mk_rows(sc0, rms=False)
                xb = self.sb("p0_xb", [128, T], BF16, sc0)
                sq = self.sb("p0_sq", [128, T], BF16, sc0)
                xf = [self.sb(f"p0_xf{i}", [128, T], F32, sc0) for i in range(2)]
                st_done = self.stats_pass(self.x0_img, ps_sum, ps_ssq, xb, sq)
                bc_done = self.rows_bcast(ps_sum, ps_ssq, st_done, R, self.ps_bc)
                self.ln_ready = self.normalize(self.x0_img, R, bc_done, xf)
                if self.tap == "ln1":
                    self.p.emit("gpsimd", lambda e: e.dma_start(
                        out=self.tapb_img, in_=self.act_buf[:, :]),
                        wait=((self.s_dv, self.ln_ready),), inc=(self.s_st, 16))
                    self.p.emit("gpsimd", lambda e: e.dma_start(
                        out=self.tap_img[:, 0:T], in_=R["mean_b"][:, :]),
                        wait=((self.s_dv, self.ln_ready),), inc=(self.s_st, 16))
                    self.p.emit("gpsimd", lambda e: e.dma_start(
                        out=self.tap_img[:, T:2 * T], in_=R["rstd_b"][:, :]),
                        wait=((self.s_dv, self.ln_ready),), inc=(self.s_st, 16))

            self.ffn_bank_free = {}
            st1_done, ffn1_st = self.ffn(self.x0_img, self.x1_img, True,
                                         ps_sum, ps_ssq, tag="f1")

            sc1 = ExitStack()
            with sc1:
                R = self.mk_rows(sc1, rms=False)
                xf = [self.sb(f"p1_xf{i}", [128, T], F32, sc1) for i in range(2)]
                bc_done = self.rows_bcast(ps_sum, ps_ssq, st1_done, R, self.ps_bc)
                self.ln_ready = self.normalize(self.x1_img, R, bc_done, xf,
                                               st_barrier=ffn1_st)

        scB = ExitStack()
        with scB:
            self.ps_mm = [self.psum(f"pb_mm{i}", [128, 512], F32, scB)
                          for i in range(2)]
            self.qkv_prod()

        u_done = self.attn_core()
        cc_done = self.collective(u_done)

        scD = ExitStack()
        with scD:
            self.ps_mm = [self.psum(f"pd_mm{i}", [128, 512], F32, scD)
                          for i in range(2)]
            self.gate()

        scE = ExitStack()
        with scE:
            rb = {}
            rb["rstd_b"] = self.sb("r_bcast", [128, T], F32, scE)
            scE2 = ExitStack()
            with scE2:
                psE = ExitStack()
                with psE:
                    self.ps_bc = self.psum("p5_bc", [128, 512], F32, psE)
                    ps_ss = [self.psum(f"p5_ss{i}", [128, 512], F32, psE)
                             for i in range(2)]
                    ps_corr = [self.psum(f"p5_c{i}", [128, 256], F32, psE)
                               for i in range(2)]
                    ss_done = self.assemble(cc_done, ps_ss, ps_corr)
                    rb["row_ssq"] = self.sb("p5_rssq", [1, T], F32, scE2)
                    rb["row_t"] = self.sb("p5_rt", [1, T], F32, scE2)
                    rb["row_hi"] = self.sb("p5_rhi", [1, T], BF16, scE2)
                    rb["row_lo"] = self.sb("p5_rlo", [1, T], BF16, scE2)
                    rb["row_ev"] = rb["rstd_b"][:, 0:1024]
                    self.p.emit("sync", None,
                                wait=((self.s_dv, self.s_dv.n),
                                      (self.s_ac, self.s_ac.n)))
                    bc_done = self.rows_bcast(ps_ss[0], ps_ss[1], ss_done, rb,
                                              self.ps_bc, rms=True)
            self.y_ready = self.s_dv.n

            scF = ExitStack()
            with scF:
                self.ps_mm = [self.psum(f"p6_mm{i}", [128, 512], F32, scF)
                              for i in range(2)]
                self.ps_bc = self.psum("p6_bc", [128, 512], F32, scF)
                ps_sum = self.psum("p6_sum", [128, 512], F32, scF)
                ps_ssq = self.psum("p6_ssq", [128, 512], F32, scF)
                st2_done, zst = self.zproj(rb, bc_done, ps_sum, ps_ssq)

                sc2 = ExitStack()
                with sc2:
                    R = self.mk_rows(sc2, rms=False)
                    xf = [self.sb(f"p7_xf{i}", [128, T], F32, sc2) for i in range(2)]
                    bc2 = self.rows_bcast(ps_sum, ps_ssq, st2_done, R, self.ps_bc)
                    self.ln_ready = self.normalize(self.x2_img, R, bc2, xf,
                                                   st_barrier=zst)

                self.ffn_bank_free = {}
                _, ffn2_st = self.ffn(self.x2_img, self.y_img, False, tag="f2")

        if self.tap:
            src = {"x1": self.x1_img, "x2": self.x2_img}.get(self.tap)
            srcb = {"ln2": None, "gate": self.g_img, "at": self.at_img,
                    "qt": self.qt_img, "kt": self.kt_img, "v": self.v_img}.get(
                        self.tap)
            if src is not None:
                p.emit("gpsimd", lambda e: e.dma_start(out=self.tap_img, in_=src),
                       wait=self.fence(), inc=(self.s_st, 16))
            if srcb is not None:
                p.emit("gpsimd", lambda e: e.dma_start(out=self.tapb_img, in_=srcb),
                       wait=self.fence(), inc=(self.s_st, 16))

        p.emit("sync", None, wait=((self.s_st, self.s_st.n),
                                   (self.s_wl, self.s_wl.n),
                                   (self.s_xl, self.s_xl.n)))
        p.emit("gpsimd", None, wait=((self.s_st, self.s_st.n),))

        self.p.replay()
        self.ctx.close()

        self.nc.all_engine_barrier()
        self.nc.clear_and_free_semaphores([s.h for s in self.p.sems])
        self.nc.all_engine_barrier()
        return self.nc

    _rows_id = 0

    def mk_rows(self, cx, rms=False):
        Builder._rows_id += 1
        i = Builder._rows_id
        R = {
            "row_sum": self.sb(f"r{i}_sum", [1, T], F32, cx),
            "row_ssq": self.sb(f"r{i}_ssq", [1, T], F32, cx),
            "row_t": self.sb(f"r{i}_t", [1, T], F32, cx),
            "row_hi": self.sb(f"r{i}_hi", [1, T], BF16, cx),
            "row_lo": self.sb(f"r{i}_lo", [1, T], BF16, cx),
            "mean_b": self.sb(f"r{i}_meanb", [128, T], F32, cx),
            "rstd_b": self.sb(f"r{i}_rstdb", [128, T], F32, cx),
        }
        R["row_ev"] = R["mean_b"][:, 0:1024]
        return R



def _stage_inputs(L0_attn_hidden_states, L0_attn_qkv_w, L0_attn_out_w,
                  L0_attn_gate_w, L0_attn_norm_w, L0_ffn_route_w, L0_ffn_A,
                  L0_ffn_Bw):
    x = np.asarray(L0_attn_hidden_states, dtype=np.float32)
    qkv_w = np.asarray(L0_attn_qkv_w, dtype=np.float32)
    out_w = np.asarray(L0_attn_out_w, dtype=np.float32)
    gate_w = np.asarray(L0_attn_gate_w, dtype=np.float32)
    norm_w = np.asarray(L0_attn_norm_w, dtype=np.float32)
    route_w = np.asarray(L0_ffn_route_w, dtype=np.float32)
    A = np.asarray(L0_ffn_A, dtype=np.float32)
    Bw = np.asarray(L0_ffn_Bw, dtype=np.float32)

    qkv3 = qkv_w.reshape(H, 3, HD, D)
    Wq = qkv3[:, 0].reshape(H * HD, D)
    Wk = qkv3[:, 1].reshape(H * HD, D)
    Wv = qkv3[:, 2].reshape(H * HD, D)

    shared = {
        "wq_img": _img_lhsT(Wq.T.copy()),
        "wk_img": _img_lhsT(Wk.T.copy()),
        "wv_img": _img_lhsT(Wv.T.copy(), ocw=512),
        "wg_img": _img_lhsT(gate_w.T.copy()),
        "wo_img": _img_lhsT(out_w.T.copy()),
        "a_img": _img_lhsT(A.reshape(ER, D).T.copy(), ocw=ER),
        "bw_img": np.ascontiguousarray(Bw.transpose(0, 2, 1).reshape(ER, D)),
        "rt_img": _img_lhsT(route_w.T.copy(), ocw=E),
        "nw_img": np.ascontiguousarray(norm_w.reshape(H, 128).T),
        "qd_img": np.ascontiguousarray(
            np.broadcast_to(QD_TAB[:, None, :], (H, 128, BS)
                            ).transpose(1, 0, 2).reshape(128, H * BS)),
        "kd_img": np.ascontiguousarray(
            KD_TAB.reshape(H, 2, 128).transpose(2, 0, 1).reshape(128, H * 2)),
        "mk_img": np.ascontiguousarray(
            MASKT_TAB.reshape(H, 2, 128, BS).transpose(2, 0, 1, 3
                                                       ).reshape(128, H * 2 * BS)),
        "em_img": np.ascontiguousarray(
            (np.arange(ER) // 16 == np.arange(E)[:, None]).astype(BFnp)),
        "id16": np.eye(128, dtype=BFnp),
        "idf32": np.eye(128, dtype=np.float32),
        "id8x": np.ascontiguousarray(
            np.tile(np.eye(32, 8, dtype=np.float32), (4, 1))),
        "onec": np.ones((128, 1), dtype=BFnp),
        "oner": np.ones((1, 128), dtype=BFnp),
    }

    in_maps = []
    for c in range(NCORE):
        b, q = c // 4, c % 4
        xs = x[b, q * T:(q + 1) * T, :]
        cc = np.zeros((NCORE, H), dtype=np.float32)
        for r in range(NCORE):
            rb_, rq = r // 4, r % 4
            if rb_ == b and rq < q:
                cc[r] = (BD_TAB.astype(np.float64) ** (NB * (q - 1 - rq))
                         ).astype(np.float32)
        m = dict(shared)
        m["x0_img"] = _img_act(xs)
        m["cc_img"] = np.ascontiguousarray(
            np.broadcast_to(cc.reshape(1, NCORE * H), (128, NCORE * H)))
        in_maps.append(m)
    return in_maps


def _unstage_output(res):
    outs = []
    for c in range(NCORE):
        img = np.asarray(res[c]["y_img"])
        xs = img.reshape(128, KC, T).transpose(1, 0, 2).reshape(D, T).T
        outs.append(xs)
    full = np.stack(outs, 0).reshape(2, 4 * T, D)
    return np.ascontiguousarray(full.astype(np.float32))


_BUILD_CACHE = {}


def kernel(**inputs):
    in_maps = _stage_inputs(**inputs)
    nc = Builder().build()
    res = run_bass_kernel_spmd(nc, in_maps, core_ids=list(range(NCORE)))
    return _unstage_output(res.results)


# version 02
# speedup vs baseline: 1.0229x; 1.0229x over previous
import numpy as np
import ml_dtypes
from contextlib import ExitStack

import concourse.bass as bass
import concourse.mybir as mybir
from concourse.bass_utils import run_bass_kernel_spmd

F32 = mybir.dt.float32
BF16 = mybir.dt.bfloat16
AF = mybir.ActivationFunctionType
ALU = mybir.AluOpType
AXX = mybir.AxisListType.X

D = 4096
KC = 32
T = 2048
NT = 4
NW = 512
H = 32
HD = 128
BS = 256
NB = 8
E = 8
ER = 128
LN_EPS = 1e-5
RMS_EPS = 1e-5
NCORE = 8

BFnp = ml_dtypes.bfloat16


def _decay_tables():
    h = np.arange(H, dtype=np.float64)
    base = 1.0 / 2.0 ** (8.0 / H)
    factor = 1.0 - 1.0 / (H - 1 + 1e-05) + 1e-05
    slope = (base ** (h + 1.0)) * factor
    r = np.arange(BS, dtype=np.float64) + 1.0
    qd = np.exp(-slope[:, None] * r[None, :])
    kd = np.exp(-slope[:, None] * (BS - r)[None, :])
    diff = r[:, None] - r[None, :]
    mask = np.where(diff >= 0,
                    np.exp(-np.minimum(slope[:, None, None] * diff[None], 500.0)),
                    0.0)
    maskT = mask.transpose(0, 2, 1).copy()
    bd = np.exp(-slope * BS)
    return (qd.astype(np.float32), kd.astype(np.float32),
            maskT.astype(np.float32), bd.astype(np.float32))


QD_TAB, KD_TAB, MASKT_TAB, BD_TAB = _decay_tables()


def _img_lhsT(WT, ocw=128):
    DI, DO = WT.shape
    kc, oc = DI // 128, DO // ocw
    return np.ascontiguousarray(
        WT.reshape(kc, 128, oc, ocw).transpose(1, 2, 0, 3).reshape(128, oc * kc * ocw))


def _img_act(x):
    t, d = x.shape
    return np.ascontiguousarray(
        x.T.reshape(d // 128, 128, t).transpose(1, 0, 2).reshape(128, -1))


ENGS = ("tensor", "vector", "scalar", "gpsimd", "sync")


class Sem:
    def __init__(self, nc, name):
        self.h = nc.alloc_semaphore(name=name)
        self.n = 0


class Prog:
    def __init__(self, nc):
        self.nc = nc
        self.streams = {e: [] for e in ENGS}
        self.last_wait = {e: {} for e in ENGS}
        self.sems = []

    def sem(self, name):
        s = Sem(self.nc, name)
        self.sems.append(s)
        return s

    def emit(self, eng, fn, wait=(), inc=None):
        lw = self.last_wait[eng]
        real = []
        for s, v in wait:
            if v <= 0 or lw.get(s, 0) >= v:
                continue
            assert v <= s.n, f"[{eng}] wait {v} > scheduled {s.n} for sem"
            lw[s] = v
            real.append((s, v))
        if inc is not None:
            inc[0].n += inc[1]
        if fn is None and not real:
            return
        self.streams[eng].append((real, fn, inc))

    def replay(self):
        with self.nc.Block() as block:
            for eng in ENGS:
                stream = self.streams[eng]

                def runner(e, stream=stream):
                    for waits, fn, inc in stream:
                        for s, v in waits:
                            e.wait_ge(s.h, v)
                        if fn is not None:
                            ins = fn(e)
                            if inc is not None:
                                assert ins is not None
                                ins.then_inc(inc[0].h, inc[1])
                getattr(block, eng)(runner)


class Builder:
    def __init__(self, tap=None):
        nc = bass.Bass("TRN2", target_bir_lowering=False)
        self.nc = nc
        self.tap = tap
        self.p = Prog(nc)
        p = self.p

        def ein(name, shape, dt):
            return nc.dram_tensor(name, shape, dt, kind="ExternalInput").ap()

        self.x0_img = ein("x0_img", [128, KC * T], F32)
        self.cc_img = ein("cc_img", [128, NCORE * H], F32)
        self.wq_img = ein("wq_img", [128, H * KC * HD], F32)
        self.wk_img = ein("wk_img", [128, H * KC * HD], F32)
        self.wv_img = ein("wv_img", [128, 8 * KC * 512], F32)
        self.wg_img = ein("wg_img", [128, KC * KC * 128], F32)
        self.wo_img = ein("wo_img", [128, KC * KC * 128], F32)
        self.a_img = ein("a_img", [128, KC * ER], F32)
        self.bw_img = ein("bw_img", [128, KC * 128], F32)
        self.rt_img = ein("rt_img", [128, KC * E], F32)
        self.nw_img = ein("nw_img", [128, H], F32)
        self.qd_img = ein("qd_img", [128, H * BS], F32)
        self.kd_img = ein("kd_img", [128, H * 2], F32)
        self.mk_img = ein("mk_img", [128, H * 2 * BS], F32)
        self.em_img = ein("em_img", [8, ER], BF16)
        self.id16 = ein("id16", [128, 128], BF16)
        self.idf32 = ein("idf32", [128, 128], F32)
        self.id8x = ein("id8x", [128, 8], F32)
        self.onec = ein("onec", [128, 1], BF16)
        self.oner = ein("oner", [1, 128], BF16)

        self.y_img = nc.dram_tensor("y_img", [128, KC * T], F32,
                                    kind="ExternalOutput").ap()
        if tap:
            self.tap_img = nc.dram_tensor("tap_img", [128, KC * T], F32,
                                          kind="ExternalOutput").ap()
            self.tapb_img = nc.dram_tensor("tapb_img", [128, KC * T], BF16,
                                           kind="ExternalOutput").ap()

        self.x1_img = nc.dram_tensor("x1_img", [128, KC * T], F32, kind="Internal").ap()
        self.x2_img = nc.dram_tensor("x2_img", [128, KC * T], F32, kind="Internal").ap()
        self.qt_img = nc.dram_tensor("qt_img", [128, H * T], BF16, kind="Internal").ap()
        self.kt_img = nc.dram_tensor("kt_img", [128, H * T], BF16, kind="Internal").ap()
        self.v_img = nc.dram_tensor("v_img", [128, H * T], BF16, kind="Internal").ap()
        self.at_img = nc.dram_tensor("at_img", [128, H * T], BF16, kind="Internal").ap()
        self.g_img = nc.dram_tensor("g_img", [128, KC * T], BF16, kind="Internal").ap()
        self.u_in = nc.dram_tensor("u_in", [H * HD, HD], F32, kind="Internal").ap()
        self.u_out = nc.dram_tensor("u_out", [NCORE * H * HD, HD], F32,
                                    kind="Internal").ap()

        self.s_wl = p.sem("s_wl")
        self.s_xl = p.sem("s_xl")
        self.s_pe = p.sem("s_pe")
        self.s_dv = p.sem("s_dv")
        self.s_ac = p.sem("s_ac")
        self.s_pl = p.sem("s_pl")
        self.s_st = p.sem("s_st")
        self.s_cc = p.sem("s_cc")

        self.ctx = ExitStack()

    def sb(self, name, shape, dt, cx=None):
        return (cx or self.ctx).enter_context(self.nc.sbuf_tensor(name, shape, dt))

    def psum(self, name, shape, dt=F32, cx=None):
        return (cx or self.ctx).enter_context(self.nc.psum_tensor(name, shape, dt))

    def AB(self, kc, lo=0, hi=T):
        return self.act_buf[:, kc * T + lo: kc * T + hi]

    def fence(self):
        return ((self.s_pe, self.s_pe.n), (self.s_dv, self.s_dv.n),
                (self.s_ac, self.s_ac.n), (self.s_pl, self.s_pl.n),
                (self.s_st, self.s_st.n))

    def alloc_globals(self):
        self.act_buf = self.sb("act_buf", [128, KC * T], BF16)
        self.id16_sb = self.sb("id16_sb", [128, 128], BF16)
        self.idf32_sb = self.sb("idf32_sb", [128, 128], F32)
        self.id8x_sb = self.sb("id8x_sb", [128, 8], F32)
        self.onec_sb = self.sb("onec_sb", [128, 1], BF16)
        self.oner_sb = self.sb("oner_sb", [1, 128], BF16)
        self.nw_sb = self.sb("nw_sb", [128, H], F32)
        self.cc_sb = self.sb("cc_sb", [128, NCORE * H], F32)
        self.kd_sb = self.sb("kd_sb", [128, H * 2], F32)
        self.em_sb = self.sb("em_sb", [8, ER], BF16)

        pairs = ((self.id16_sb, self.id16), (self.idf32_sb, self.idf32), (self.id8x_sb, self.id8x),
                 (self.onec_sb, self.onec), (self.oner_sb, self.oner),
                 (self.nw_sb, self.nw_img), (self.cc_sb, self.cc_img),
                 (self.kd_sb, self.kd_img), (self.em_sb, self.em_img))
        for dst, src in pairs:
            self.p.emit("gpsimd",
                        lambda e, dst=dst, src=src: e.dma_start(out=dst[:, :], in_=src),
                        inc=(self.s_wl, 16))
        self.consts_ready = self.s_wl.n

    def zero_stats(self, ps_sum, ps_ssq):
        self.p.emit("vector", lambda e: e.memset(ps_sum[:, :], 0.0))
        self.p.emit("vector", lambda e: e.memset(ps_ssq[:, :], 0.0),
                    inc=(self.s_dv, 1))
        return self.s_dv.n

    def stats_mm(self, bank, src, nt, stop):
        def fn(e, bank=bank, src=src, nt=nt, stop=stop):
            return e.matmul(bank[32 * nt:32 * nt + 1, :], self.onec_sb[:, :], src,
                            start=False, stop=stop, skip_group_check=True,
                            tile_position=(0, 32 * nt))
        return fn

    def rows_bcast(self, ps_sum, ps_ssq, pe_done, R, ps_bc, rms=False):
        p, s_pe, s_dv, s_ac, s_st = self.p, self.s_pe, self.s_dv, self.s_ac, self.s_st
        row_t, row_hi, row_lo = R["row_t"], R["row_hi"], R["row_lo"]
        row_ev = R["row_ev"]
        p.emit("vector", lambda e: e.tensor_copy(row_ev[:, 0:512], ps_sum[:, :]),
               wait=((s_pe, pe_done),))
        p.emit("vector", lambda e: e.tensor_copy(row_ev[:, 512:1024], ps_ssq[:, :]),
               inc=(s_dv, 1))
        ev_done = s_dv.n

        if rms:
            def gather(e):
                for j in range(8):
                    src = row_ev[32 * (j % 4):32 * (j % 4) + 1,
                                 (0 if j < 4 else 512):(0 if j < 4 else 512) + 256]
                    e.dma_start(out=R["row_ssq"][0:1, j * 256:(j + 1) * 256],
                                in_=src).then_inc(self.s_st.h, 16)
            ndma = 8
        else:
            def gather(e):
                for nt in range(NT):
                    e.dma_start(out=R["row_sum"][0:1, nt * NW:(nt + 1) * NW],
                                in_=row_ev[32 * nt:32 * nt + 1, 0:512]
                                ).then_inc(self.s_st.h, 16)
                    e.dma_start(out=R["row_ssq"][0:1, nt * NW:(nt + 1) * NW],
                                in_=row_ev[32 * nt:32 * nt + 1, 512:1024]
                                ).then_inc(self.s_st.h, 16)
            ndma = 8
        p.emit("sync", gather, wait=((s_dv, ev_done),))
        s_st.n += 16 * ndma
        g_done = s_st.n

        if rms:
            p.emit("vector", lambda e: e.tensor_scalar(
                row_t[0:1, :], R["row_ssq"][0:1, :], 1.0 / D, RMS_EPS,
                ALU.mult, ALU.add), wait=((s_st, g_done),), inc=(s_dv, 1))
        else:
            p.emit("vector", lambda e: e.tensor_scalar(
                R["row_sum"][0:1, :], R["row_sum"][0:1, :], 1.0 / D, None,
                ALU.mult), wait=((s_st, g_done),))
            p.emit("vector", lambda e: e.tensor_scalar(
                row_t[0:1, :], R["row_ssq"][0:1, :], 1.0 / D, LN_EPS,
                ALU.mult, ALU.add))
            p.emit("vector", lambda e: e.tensor_mul(
                R["row_ssq"][0:1, :], R["row_sum"][0:1, :], R["row_sum"][0:1, :]))
            p.emit("vector", lambda e: e.tensor_sub(
                row_t[0:1, :], row_t[0:1, :], R["row_ssq"][0:1, :]),
                inc=(s_dv, 1))
        t_done = s_dv.n
        p.emit("scalar", lambda e: e.activation(row_t[0:1, :], row_t[0:1, :], AF.Sqrt),
               wait=((s_dv, t_done),), inc=(s_ac, 1))
        p.emit("vector", lambda e: e.reciprocal(row_t[0:1, :], row_t[0:1, :]),
               wait=((s_ac, s_ac.n),))

        bc_free = {}

        def broadcast(src_row, dst, tag):
            p.emit("vector", lambda e: e.tensor_copy(row_hi[0:1, :], src_row[0:1, :]))
            p.emit("vector", lambda e: e.tensor_sub(row_lo[0:1, :], src_row[0:1, :],
                                                    row_hi[0:1, :]),
                   inc=(s_dv, 1))
            rows_done = s_dv.n
            for nt in range(NT):
                p.emit("tensor", lambda e, nt=nt: e.matmul(
                    ps_bc[:, :], self.oner_sb[0:1, :],
                    row_hi[0:1, nt * NW:(nt + 1) * NW], start=True, stop=False),
                    wait=((s_dv, rows_done), (s_dv, bc_free.get("f", 0))))
                p.emit("tensor", lambda e, nt=nt: e.matmul(
                    ps_bc[:, :], self.oner_sb[0:1, :],
                    row_lo[0:1, nt * NW:(nt + 1) * NW], start=False, stop=True),
                    inc=(s_pe, 1))
                mm_done = s_pe.n
                p.emit("vector", lambda e, nt=nt, dst=dst: e.tensor_copy(
                    dst[:, nt * NW:(nt + 1) * NW], ps_bc[:, :]),
                    wait=((s_pe, mm_done),), inc=(s_dv, 1))
                bc_free["f"] = s_dv.n
            return s_dv.n

        done = broadcast(row_t, R["rstd_b"], "r")
        if not rms:
            done = broadcast(R["row_sum"], R["mean_b"], "m")
        return done

    def stats_pass(self, src_img, ps_sum, ps_ssq, xb, sq, extra_wait=()):
        p = self.p
        z = self.zero_stats(ps_sum, ps_ssq)
        pe_after = {0: 0}
        for kc in range(KC):
            p.emit("gpsimd", lambda e, kc=kc: e.dma_start(
                out=xb[:, :], in_=src_img[:, kc * T:(kc + 1) * T]),
                wait=((self.s_pe, pe_after.get(kc, 0)),
                      (self.s_ac, pe_after.get(("a", kc), 0)),
                      (self.s_wl, self.consts_ready)) + tuple(extra_wait),
                inc=(self.s_wl, 16))
            ld = self.s_wl.n
            p.emit("scalar", lambda e: e.activation(sq[:, :], xb[:, :], AF.Square),
                   wait=((self.s_wl, ld), (self.s_pe, pe_after.get(kc, 0))),
                   inc=(self.s_ac, 1))
            sqd = self.s_ac.n
            pe_after[("a", kc + 1)] = sqd
            for nt in range(NT):
                p.emit("tensor", self.stats_mm(
                    ps_sum, xb[:, nt * NW:(nt + 1) * NW], nt, kc == KC - 1),
                    wait=((self.s_wl, ld), (self.s_dv, z)))
                p.emit("tensor", self.stats_mm(
                    ps_ssq, sq[:, nt * NW:(nt + 1) * NW], nt, kc == KC - 1),
                    wait=((self.s_ac, sqd),),
                    inc=(self.s_pe, 1) if nt == NT - 1 else None)
            pe_after[kc + 1] = self.s_pe.n
        return self.s_pe.n

    def normalize(self, src_img, R, bc_done, xf, st_barrier=0):
        p = self.p
        fence = self.fence()
        mul_done = {}
        for kc in range(KC):
            sl = kc % 2
            xf_w = ((self.s_dv, mul_done.get(sl, 0)), (self.s_st, st_barrier))
            if kc < 2:
                xf_w = xf_w + fence
            p.emit("sync", lambda e, kc=kc, sl=sl: e.dma_start(
                out=xf[sl][:, :], in_=src_img[:, kc * T:(kc + 1) * T]),
                wait=xf_w, inc=(self.s_xl, 16))
            ld = self.s_xl.n
            p.emit("gpsimd", lambda e, sl=sl: e.tensor_sub(
                xf[sl][:, :], xf[sl][:, :], R["mean_b"][:, :]),
                wait=((self.s_xl, ld), (self.s_dv, bc_done)),
                inc=(self.s_pl, 1))
            subd = self.s_pl.n
            p.emit("vector", lambda e, kc=kc, sl=sl: e.tensor_mul(
                self.AB(kc), xf[sl][:, :], R["rstd_b"][:, :]),
                wait=((self.s_pl, subd),), inc=(self.s_dv, 1))
            mul_done[sl] = self.s_dv.n
        return self.s_dv.n

    def fmm(self, w_img, n_oc, slabs, ln_ready, evac, slab_cols=KC * 128,
            extra_slab_wait=()):
        p = self.p
        nslab = len(slabs)
        slab_free = {}
        bank_free = {}
        ps_mm = self.ps_mm
        for oc in range(n_oc):
            sl = oc % nslab
            p.emit("gpsimd", lambda e, oc=oc, sl=sl: e.dma_start(
                out=slabs[sl][:, :],
                in_=w_img[:, oc * slab_cols:(oc + 1) * slab_cols]),
                wait=(slab_free.get(sl, (self.s_pe, 0)),) + tuple(extra_slab_wait),
                inc=(self.s_wl, 16))
            ld = self.s_wl.n
            for nt in range(NT):
                bk = (oc * NT + nt) % 2
                for kc in range(KC):
                    wt = [(self.s_wl, ld), (self.s_dv, ln_ready)]
                    if kc == 0:
                        wt.append(bank_free.get(bk, (self.s_pe, 0)))
                    p.emit("tensor", lambda e, sl=sl, kc=kc, nt=nt, bk=bk: e.matmul(
                        ps_mm[bk][:, :],
                        slabs[sl][:, kc * 128:(kc + 1) * 128],
                        self.AB(kc, nt * NW, (nt + 1) * NW),
                        start=(kc == 0), stop=(kc == KC - 1)),
                        wait=tuple(wt),
                        inc=(self.s_pe, 1) if kc == KC - 1 else None)
                mm_done = self.s_pe.n
                bank_free[bk] = evac(oc, nt, ps_mm[bk], mm_done)
            slab_free[sl] = (self.s_pe, self.s_pe.n)
        return

    def ffn(self, x_old_img, x_new_img, with_stats, ps_sum=None, ps_ssq=None,
            tag="f"):
        p = self.p
        s_pe, s_dv, s_ac, s_wl, s_xl, s_st = (self.s_pe, self.s_dv, self.s_ac,
                                              self.s_wl, self.s_xl, self.s_st)
        sc = ExitStack()
        with sc:
            a_slab = self.sb(tag + "a_slab", [128, KC * ER], BF16, sc)
            bw_slab = self.sb(tag + "bw_slab", [128, KC * 128], BF16, sc)
            rt_slab = self.sb(tag + "rt_slab", [128, KC * E], BF16, sc)
            lg_sb = self.sb(tag + "lg_sb", [128, 1024], F32, sc)
            lgt_sb = self.sb(tag + "lgt_sb", [128, 128], F32, sc)
            mx_sb = self.sb(tag + "mx_sb", [128, 16], F32, sc)
            rw_sb = self.sb(tag + "rw_sb", [128, 128], F32, sc)
            rwh_sb = self.sb(tag + "rwh_sb", [128, 128], BF16, sc)
            rwl_sb = self.sb(tag + "rwl_sb", [128, 128], BF16, sc)
            rwt_hi = self.sb(tag + "rwt_hi", [8, T], BF16, sc)
            rwt_lo = self.sb(tag + "rwt_lo", [8, T], BF16, sc)
            h_sb = self.sb(tag + "h_sb", [128, T], BF16, sc)
            wh_sb = self.sb(tag + "wh_sb", [128, T], BF16, sc)
            xo = [self.sb(tag + f"xo{i}", [128, T], F32, sc) for i in range(2)]
            xn = self.sb(tag + "xn", [128, T], F32, sc)
            if with_stats:
                xb = self.sb(tag + "xb", [128, T], BF16, sc)
                sqb = self.sb(tag + "sqb", [128, T], BF16, sc)
            psc = ExitStack()
            with psc:
                ps_lg = [self.psum(tag + f"ps_lg{i}", [128, 512], F32, psc)
                         for i in range(2)]
                ps_loc = self.psum(tag + "ps_loc", [8, 1024], BF16, psc)

                fence = self.fence()
                ln_ready = self.ln_ready

                for dst, src in ((a_slab, self.a_img), (bw_slab, self.bw_img),
                                 (rt_slab, self.rt_img)):
                    p.emit("gpsimd", lambda e, dst=dst, src=src: e.dma_start(
                        out=dst[:, :], in_=src), wait=fence, inc=(s_wl, 16))
                slabs_ld = s_wl.n

                for nt in range(NT):
                    bank = ps_lg[nt // 2]
                    cg = 32 * (nt % 2)
                    for kc in range(KC):
                        p.emit("tensor", lambda e, nt=nt, kc=kc, bank=bank, cg=cg:
                               e.matmul(bank[cg:cg + 8, 0:512],
                                        rt_slab[:, kc * E:(kc + 1) * E],
                                        self.AB(kc, nt * NW, (nt + 1) * NW),
                                        start=(kc == 0), stop=(kc == KC - 1),
                                        tile_position=(0, cg)),
                               wait=((s_wl, slabs_ld), (s_dv, ln_ready)),
                               inc=(s_pe, 1) if kc == KC - 1 else None)
                lg_done = s_pe.n

                bank_free = {}
                for nt in range(NT):
                    bk = nt % 2
                    for kc in range(KC):
                        wt = [(s_wl, slabs_ld), (s_dv, ln_ready)]
                        if kc == 0:
                            wt.append(bank_free.get(bk, (s_pe, 0)))
                        p.emit("tensor", lambda e, nt=nt, kc=kc, bk=bk: e.matmul(
                            self.ps_mm[bk][:, :],
                            a_slab[:, kc * ER:(kc + 1) * ER],
                            self.AB(kc, nt * NW, (nt + 1) * NW),
                            start=(kc == 0), stop=(kc == KC - 1)),
                            wait=tuple(wt),
                            inc=(s_pe, 1) if kc == KC - 1 else None)
                    hmm = s_pe.n
                    p.emit("vector", lambda e, nt=nt, bk=bk: e.tensor_copy(
                        h_sb[:, nt * NW:(nt + 1) * NW], self.ps_mm[bk][:, :]),
                        wait=((s_pe, hmm),), inc=(s_dv, 1))
                    bank_free[bk] = (s_dv, s_dv.n)
                h_done = s_dv.n

                p.emit("vector", lambda e: e.tensor_copy(lg_sb[:, 0:512],
                                                         ps_lg[0][:, :]),
                       wait=((s_pe, lg_done),))
                p.emit("vector", lambda e: e.tensor_copy(lg_sb[:, 512:1024],
                                                         ps_lg[1][:, :]),
                       inc=(s_dv, 1))
                lg_ev = s_dv.n
                for tc in range(16):
                    nt, sub = tc // 4, tc % 4
                    cg = 32 * (nt % 2)
                    src = lg_sb[cg:cg + 8,
                                (nt // 2) * 512 + sub * 128:
                                (nt // 2) * 512 + (sub + 1) * 128]
                    idp = self.id8x_sb[cg:cg + 8, 0:8]
                    p.emit("tensor", lambda e, tc=tc, src=src, idp=idp: e.transpose(
                        self.ps_bc[:, tc * 8:(tc + 1) * 8], src, idp),
                        wait=((s_dv, lg_ev),),
                        inc=(s_pe, 1) if tc == 15 else None)
                tr_done = s_pe.n
                p.emit("vector", lambda e: e.tensor_copy(lgt_sb[:, :],
                                                         self.ps_bc[:, 0:128]),
                       wait=((s_pe, tr_done),), inc=(s_dv, 1))
                bcf = s_dv.n

                for tc in range(16):
                    p.emit("vector", lambda e, tc=tc: e.reduce_max(
                        mx_sb[:, tc:tc + 1], lgt_sb[:, tc * 8:(tc + 1) * 8],
                        axis=AXX))
                for tc in range(16):
                    mxb = bass.AP(mx_sb, tc, [[16, 128], [0, 8]])
                    p.emit("vector", lambda e, tc=tc, mxb=mxb: e.tensor_tensor(
                        rw_sb[:, tc * 8:(tc + 1) * 8],
                        lgt_sb[:, tc * 8:(tc + 1) * 8], mxb, ALU.is_equal))
                for tc in range(16):
                    mxb = bass.AP(mx_sb, tc, [[16, 128], [0, 8]])
                    p.emit("vector", lambda e, tc=tc, mxb=mxb: e.tensor_tensor(
                        rw_sb[:, tc * 8:(tc + 1) * 8],
                        rw_sb[:, tc * 8:(tc + 1) * 8], mxb, ALU.mult))
                p.emit("vector", lambda e: e.tensor_copy(rwh_sb[:, :], rw_sb[:, :]))
                p.emit("vector", lambda e: e.tensor_sub(rwl_sb[:, :], rw_sb[:, :],
                                                        rwh_sb[:, :]),
                       inc=(s_dv, 1))
                rw_done = s_dv.n

                loc_free = 0
                rwt_done = 0
                for half in range(2):
                    for which, (src, dst) in enumerate(((rwh_sb, rwt_hi),
                                                       (rwl_sb, rwt_lo))):
                        for i in range(8):
                            tc = half * 8 + i
                            p.emit("tensor", lambda e, i=i, tc=tc, src=src: e.transpose(
                                ps_loc[0:8, i * 128:(i + 1) * 128],
                                src[:, tc * 8:(tc + 1) * 8], self.id16_sb[:, :]),
                                wait=((s_dv, rw_done), (s_dv, loc_free)),
                                inc=(s_pe, 1) if i == 7 else None)
                        trd = s_pe.n
                        p.emit("vector", lambda e, half=half, dst=dst: e.tensor_copy(
                            dst[0:8, half * 1024:(half + 1) * 1024],
                            ps_loc[0:8, :]),
                            wait=((s_pe, trd),), inc=(s_dv, 1))
                        loc_free = s_dv.n
                rwt_done = s_dv.n

                whd = {}
                for nt in range(NT):
                    p.emit("tensor", lambda e, nt=nt: e.matmul(
                        self.ps_bc[:, :], self.em_sb[0:8, :],
                        rwt_hi[0:8, nt * NW:(nt + 1) * NW], start=True, stop=False),
                        wait=((s_dv, rwt_done), (s_dv, bcf),
                              (s_dv, whd.get("bc", 0))))
                    p.emit("tensor", lambda e, nt=nt: e.matmul(
                        self.ps_bc[:, :], self.em_sb[0:8, :],
                        rwt_lo[0:8, nt * NW:(nt + 1) * NW], start=False, stop=True),
                        inc=(s_pe, 1))
                    rwe = s_pe.n
                    p.emit("vector", lambda e, nt=nt: e.tensor_mul(
                        wh_sb[:, nt * NW:(nt + 1) * NW],
                        h_sb[:, nt * NW:(nt + 1) * NW], self.ps_bc[:, :]),
                        wait=((s_pe, rwe), (s_dv, h_done)), inc=(s_dv, 1))
                    whd["bc"] = s_dv.n
                wh_done = s_dv.n

                if self.tap == "ffn1" and tag == "f1":
                    w = ((s_pe, s_pe.n), (s_dv, s_dv.n), (s_ac, s_ac.n))
                    for dst_lo, src in ((0, h_sb[:, :]), (2048, wh_sb[:, :]),
                                        (6144, rwh_sb[:, :]), (6272, rwl_sb[:, :])):
                        p.emit("gpsimd", lambda e, dst_lo=dst_lo, src=src:
                               e.dma_start(out=self.tapb_img[
                                   0:src.shape[0], dst_lo:dst_lo + src.shape[1]],
                                   in_=src), wait=w, inc=(s_st, 16))
                    p.emit("gpsimd", lambda e: e.dma_start(
                        out=self.tapb_img[0:8, 8192:8192 + 2048],
                        in_=rwt_hi[0:8, :]), wait=w, inc=(s_st, 16))
                    p.emit("gpsimd", lambda e: e.dma_start(
                        out=self.tapb_img[0:8, 10240:10240 + 2048],
                        in_=rwt_lo[0:8, :]), wait=w, inc=(s_st, 16))
                    for dst_lo, src in ((0, lg_sb[:, :]), (1024, lgt_sb[:, :]),
                                        (1152, rw_sb[:, :]), (1280, mx_sb[:, :])):
                        p.emit("gpsimd", lambda e, dst_lo=dst_lo, src=src:
                               e.dma_start(out=self.tap_img[
                                   0:128, dst_lo:dst_lo + src.shape[1]],
                                   in_=src), wait=w, inc=(s_st, 16))

                if with_stats:
                    z = self.zero_stats(ps_sum, ps_ssq)
                xo_free = {}
                xn_free = (s_st, 0)
                cast_free = (s_pe, 0)
                pe_stats = 0
                for oc in range(KC):
                    sl = oc % 2
                    xo_w = (xo_free.get(sl, (s_dv, 0)),)
                    if oc < 2:
                        xo_w = xo_w + fence
                    p.emit("sync", lambda e, oc=oc, sl=sl: e.dma_start(
                        out=xo[sl][:, :],
                        in_=x_old_img[:, oc * T:(oc + 1) * T]),
                        wait=xo_w, inc=(s_xl, 16))
                    xold_ld = s_xl.n
                    dv_nt = 0
                    for nt in range(NT):
                        bk = nt % 2
                        wt = [(s_dv, wh_done)]
                        if True:
                            wt.append(self.ffn_bank_free.get(bk, (s_dv, 0)))
                        p.emit("tensor", lambda e, oc=oc, nt=nt, bk=bk: e.matmul(
                            self.ps_mm[bk][:, :],
                            bw_slab[:, oc * 128:(oc + 1) * 128],
                            wh_sb[:, nt * NW:(nt + 1) * NW],
                            start=True, stop=True),
                            wait=tuple(wt), inc=(s_pe, 1))
                        bmm = s_pe.n
                        p.emit("vector", lambda e, nt=nt, bk=bk, sl=sl:
                               e.scalar_tensor_tensor(
                                   xn[:, nt * NW:(nt + 1) * NW],
                                   self.ps_mm[bk][:, :], 0.5,
                                   xo[sl][:, nt * NW:(nt + 1) * NW],
                                   ALU.mult, ALU.add),
                               wait=((s_pe, bmm), (s_xl, xold_ld),
                                     (s_st, xn_free[1]), (s_ac, cast_free[1])),
                               inc=(s_dv, 1))
                        self.ffn_bank_free[bk] = (s_dv, s_dv.n)
                        dv_nt = s_dv.n
                    xo_free[sl] = (s_dv, dv_nt)
                    p.emit("sync", lambda e, oc=oc: e.dma_start(
                        out=x_new_img[:, oc * T:(oc + 1) * T], in_=xn[:, :]),
                        wait=((s_dv, dv_nt),), inc=(s_st, 16))
                    xn_free = (s_st, s_st.n)
                    if with_stats:
                        p.emit("scalar", lambda e: e.activation(
                            xb[:, :], xn[:, :], AF.Copy),
                            wait=((s_dv, dv_nt), (s_pe, pe_stats)))
                        p.emit("scalar", lambda e: e.activation(
                            sqb[:, :], xn[:, :], AF.Square), inc=(s_ac, 1))
                        cast_done = s_ac.n
                        cast_free = (s_ac, cast_done)
                        for nt in range(NT):
                            p.emit("tensor", self.stats_mm(
                                ps_sum, xb[:, nt * NW:(nt + 1) * NW], nt,
                                oc == KC - 1),
                                wait=((s_ac, cast_done), (s_dv, z)))
                            p.emit("tensor", self.stats_mm(
                                ps_ssq, sqb[:, nt * NW:(nt + 1) * NW], nt,
                                oc == KC - 1),
                                inc=(s_pe, 1) if nt == NT - 1 else None)
                        pe_stats = s_pe.n
                return (pe_stats if with_stats else s_pe.n), s_st.n

    def qkv_prod(self):
        p = self.p
        s_pe, s_dv, s_ac, s_wl, s_st = (self.s_pe, self.s_dv, self.s_ac,
                                        self.s_wl, self.s_st)
        ln_ready = self.ln_ready
        fence = self.fence()
        scv = ExitStack()
        with scv:
            vsl = [self.sb(f"vsl{i}", [128, KC * 512], BF16, scv) for i in range(2)]
            vst = [self.sb(f"vst{i}", [128, 512], BF16, scv) for i in range(2)]
            slab_free = {}
            stage_free = {}
            bank_free = {}
            for g in range(8):
                sl = g % 2
                p.emit("gpsimd", lambda e, g=g, sl=sl: e.dma_start(
                    out=vsl[sl][:, :],
                    in_=self.wv_img[:, g * KC * 512:(g + 1) * KC * 512]),
                    wait=(slab_free.get(sl, (s_pe, 0)),) + fence,
                    inc=(s_wl, 16))
                ld = self.s_wl.n
                for tc in range(16):
                    bk = tc % 2
                    for kc in range(KC):
                        wt = [(s_wl, ld), (s_dv, ln_ready)]
                        if kc == 0:
                            wt.append(bank_free.get(bk, (s_ac, 0)))
                        p.emit("tensor", lambda e, sl=sl, kc=kc, tc=tc, bk=bk:
                               e.matmul(self.ps_mm[bk][:, :],
                                        self.AB(kc, tc * 128, (tc + 1) * 128),
                                        vsl[sl][:, kc * 512:(kc + 1) * 512],
                                        start=(kc == 0), stop=(kc == KC - 1)),
                               wait=tuple(wt),
                               inc=(s_pe, 1) if kc == KC - 1 else None)
                    mmd = s_pe.n
                    st_sl = tc % 2
                    p.emit("scalar", lambda e, st_sl=st_sl, bk=bk: e.activation(
                        vst[st_sl][:, :], self.ps_mm[bk][:, :], AF.Silu),
                        wait=((s_pe, mmd), (s_st, stage_free.get(st_sl, 0))),
                        inc=(s_ac, 1))
                    bank_free[bk] = (s_ac, s_ac.n)
                    dst = self.v_img.rearrange("p (h t) -> p h t", h=H)[
                        :, 4 * g:4 * g + 4, tc * 128:(tc + 1) * 128]
                    src = vst[st_sl][:, :].rearrange("p (j d) -> p j d", j=4)
                    p.emit("scalar", lambda e, dst=dst, src=src: e.dma_start(
                        out=dst, in_=src), inc=(s_st, 16))
                    stage_free[st_sl] = s_st.n
                slab_free[sl] = (s_pe, s_pe.n)

        scq = ExitStack()
        with scq:
            qsl = [self.sb(f"qsl{i}", [128, KC * 128], BF16, scq) for i in range(3)]
            qst = [self.sb(f"qst{i}", [128, T], BF16, scq) for i in range(2)]
            fence_q = self.fence()
            slab_free = {}
            stage_free = {}
            bank_free = {}
            idx = 0
            for h in range(H):
                for w_img, dst_img in ((self.wq_img, self.qt_img),
                                       (self.wk_img, self.kt_img)):
                    sl = idx % 3
                    q_w = (slab_free.get(sl, (s_pe, 0)),)
                    if idx < 3:
                        q_w = q_w + fence_q
                    p.emit("gpsimd", lambda e, h=h, sl=sl, w_img=w_img: e.dma_start(
                        out=qsl[sl][:, :],
                        in_=w_img[:, h * KC * 128:(h + 1) * KC * 128]),
                        wait=q_w, inc=(s_wl, 16))
                    ld = self.s_wl.n
                    st_sl = idx % 2
                    for nt in range(NT):
                        bk = nt % 2
                        for kc in range(KC):
                            wt = [(s_wl, ld), (s_dv, ln_ready)]
                            if kc == 0:
                                wt.append(bank_free.get(bk, (s_ac, 0)))
                                if nt == 0:
                                    wt.append((s_st, stage_free.get(st_sl, 0)))
                            p.emit("tensor", lambda e, sl=sl, kc=kc, nt=nt, bk=bk:
                                   e.matmul(self.ps_mm[bk][:, :],
                                            qsl[sl][:, kc * 128:(kc + 1) * 128],
                                            self.AB(kc, nt * NW, (nt + 1) * NW),
                                            start=(kc == 0), stop=(kc == KC - 1)),
                                   wait=tuple(wt),
                                   inc=(s_pe, 1) if kc == KC - 1 else None)
                        mmd = s_pe.n
                        p.emit("scalar", lambda e, st_sl=st_sl, nt=nt, bk=bk:
                               e.activation(qst[st_sl][:, nt * NW:(nt + 1) * NW],
                                            self.ps_mm[bk][:, :], AF.Silu),
                               wait=((s_pe, mmd),), inc=(s_ac, 1))
                        bank_free[bk] = (s_ac, s_ac.n)
                    p.emit("scalar", lambda e, h=h, st_sl=st_sl, dst_img=dst_img:
                           e.dma_start(out=dst_img[:, h * T:(h + 1) * T],
                                       in_=qst[st_sl][:, :]),
                           inc=(s_st, 16))
                    stage_free[st_sl] = s_st.n
                    slab_free[sl] = (s_pe, s_pe.n)
                    idx += 1
        return s_st.n

    def attn_core(self):
        p = self.p
        s_pe, s_dv, s_ac, s_xl, s_st = (self.s_pe, self.s_dv, self.s_ac,
                                        self.s_xl, self.s_st)
        sc = ExitStack()
        with sc:
            qt = [self.sb(f"ac_qt{i}", [128, T], BF16, sc) for i in range(2)]
            kt = [self.sb(f"ac_kt{i}", [128, T], BF16, sc) for i in range(2)]
            vb = [self.sb(f"ac_vb{i}", [128, T], BF16, sc) for i in range(2)]
            qs = [self.sb(f"ac_qs{i}", [128, T], BF16, sc) for i in range(2)]
            mk = [self.sb(f"ac_mk{i}", [128, 512], F32, sc) for i in range(2)]
            qd = [self.sb(f"ac_qd{i}", [128, 256], F32, sc) for i in range(2)]
            scb = [self.sb(f"ac_sc{i}", [128, 512], BF16, sc) for i in range(2)]
            ksb = [self.sb(f"ac_ks{i}", [128, 256], BF16, sc) for i in range(2)]
            kv32 = [self.sb(f"ac_kv{i}", [128, 128], F32, sc) for i in range(2)]
            kvbf = [self.sb(f"ac_kb{i}", [128, 128], BF16, sc) for i in range(2)]
            ats = [self.sb(f"ac_at{i}", [128, T], BF16, sc) for i in range(2)]
            psc = ExitStack()
            with psc:
                ps_s = [self.psum(f"ac_ps_s{i}", [128, 512], F32, psc)
                        for i in range(2)]
                ps_kt = [self.psum(f"ac_ps_kt{i}", [128, 256], BF16, psc)
                         for i in range(2)]
                ps_a = [self.psum(f"ac_ps_a{i}", [128, 256], F32, psc)
                        for i in range(2)]
                ps_kv = [self.psum(f"ac_ps_kv{i}", [128, 128], F32, psc)
                         for i in range(2)]

                fence = self.fence()
                loads = {}
                head_pe = {}
                head_dv = {}
                head_ac = {}
                head_st = {}
                mask_dv = {}
                ks_ac = {}
                evac_ac = {}
                kv_dv = {}
                kvbf_dv = {}
                qs_dv = {}

                def emit_loads(h):
                    sl = h % 2
                    w = [(s_pe, head_pe.get(h - 2, 0)),
                         (s_dv, head_dv.get(h - 2, 0)),
                         (s_ac, head_ac.get(h - 2, 0))]
                    if h < 2:
                        w += list(fence)
                    for dst, src in ((qt[sl], self.qt_img), (kt[sl], self.kt_img),
                                     (vb[sl], self.v_img)):
                        p.emit("sync", lambda e, dst=dst, src=src, h=h: e.dma_start(
                            out=dst[:, :], in_=src[:, h * T:(h + 1) * T]),
                            wait=tuple(w), inc=(s_xl, 16))
                    p.emit("sync", lambda e, sl=sl, h=h: e.dma_start(
                        out=mk[sl][:, :], in_=self.mk_img[:, h * 512:(h + 1) * 512]),
                        wait=tuple(w), inc=(s_xl, 16))
                    p.emit("sync", lambda e, sl=sl, h=h: e.dma_start(
                        out=qd[sl][:, :], in_=self.qd_img[:, h * 256:(h + 1) * 256]),
                        wait=tuple(w), inc=(s_xl, 16))
                    loads[h] = s_xl.n

                emit_loads(0)
                emit_loads(1)

                bcnt = 0

                for h in range(H):
                    sl = h % 2
                    bd = float(BD_TAB[h])
                    ld = loads[h]
                    qd_rep = bass.AP(qd[sl], 0, [[256, 128], [0, NB], [1, 256]])
                    qt_v = bass.AP(qt[sl], 0, [[T, 128], [256, NB], [1, 256]])
                    qs_v = bass.AP(qs[sl], 0, [[T, 128], [256, NB], [1, 256]])
                    p.emit("vector", lambda e, qs_v=qs_v, qt_v=qt_v, qd_rep=qd_rep:
                           e.tensor_tensor(qs_v, qt_v, qd_rep, ALU.mult),
                           wait=((s_xl, ld), (s_pe, qs_dv.get(h - 2, 0))),
                           inc=(s_dv, 1))
                    qs_done = s_dv.n
                    p.emit("vector", lambda e, sl=sl: e.memset(kv32[0][:, :], 0.0),
                           wait=((s_st, head_st.get(h - 1, 0)),), inc=(s_dv, 1))
                    kv_dv[(h, -1)] = s_dv.n

                    def S(j, b):
                        for mc in range(2):
                            wt = [(s_xl, ld)]
                            if mc == 0:
                                wt.append((s_dv, mask_dv.get(b - 2, 0)))
                            p.emit("tensor", lambda e, j=j, mc=mc, b=b, sl=sl:
                                   e.matmul(ps_s[b % 2][:, mc * 256:(mc + 1) * 256],
                                            kt[sl][:, j * 256 + mc * 128:
                                                   j * 256 + (mc + 1) * 128],
                                            qt[sl][:, j * 256:(j + 1) * 256],
                                            start=True, stop=True),
                                   wait=tuple(wt),
                                   inc=(s_pe, 1) if mc == 1 else None)
                        return s_pe.n

                    def TR(j, b):
                        for mc in range(2):
                            wt = []
                            if mc == 0:
                                wt.append((s_ac, ks_ac.get(b - 2, 0)))
                            p.emit("tensor", lambda e, j=j, mc=mc, b=b, sl=sl:
                                   e.transpose(
                                       ps_kt[b % 2][:, mc * 128:(mc + 1) * 128],
                                       kt[sl][:, j * 256 + mc * 128:
                                              j * 256 + (mc + 1) * 128],
                                       self.id16_sb[:, :]),
                                   wait=tuple(wt),
                                   inc=(s_pe, 1) if mc == 1 else None)
                        return s_pe.n

                    def MASK(j, b, s_done):
                        p.emit("vector", lambda e, j=j, b=b, sl=sl: e.tensor_mul(
                            scb[b % 2][:, :], ps_s[b % 2][:, :], mk[sl][:, :]),
                            wait=((s_pe, s_done),), inc=(s_dv, 1))
                        mask_dv[b] = s_dv.n
                        return s_dv.n

                    def KS(j, b, t_done):
                        for mc in range(2):
                            p.emit("scalar", lambda e, j=j, mc=mc, b=b, h=h:
                                   e.activation(
                                       ksb[b % 2][:, mc * 128:(mc + 1) * 128],
                                       ps_kt[b % 2][:, mc * 128:(mc + 1) * 128],
                                       AF.Copy,
                                       scale=self.kd_sb[:, h * 2 + mc:h * 2 + mc + 1]),
                                   wait=((s_pe, t_done),) if mc == 0 else (),
                                   inc=(s_ac, 1) if mc == 1 else None)
                        ks_ac[b] = s_ac.n
                        return s_ac.n

                    def IXU(j, b, m_done, k_done):
                        for mc in range(2):
                            wt = [(s_dv, m_done), (s_xl, ld)]
                            if mc == 0:
                                wt.append((s_ac, evac_ac.get(b - 2, 0)))
                            p.emit("tensor", lambda e, j=j, mc=mc, b=b, sl=sl:
                                   e.matmul(ps_a[b % 2][:, :],
                                            vb[sl][:, (2 * j + mc) * 128:
                                                   (2 * j + mc + 1) * 128],
                                            scb[b % 2][:, mc * 256:(mc + 1) * 256],
                                            start=(mc == 0),
                                            stop=(mc == 1 and j == 0)),
                                   wait=tuple(wt))
                        if j > 0:
                            p.emit("tensor", lambda e, j=j, b=b, sl=sl: e.matmul(
                                ps_a[b % 2][:, :], kvbf[j % 2][:, :],
                                qs[sl][:, j * 256:(j + 1) * 256],
                                start=False, stop=True),
                                wait=((s_dv, kvbf_dv[(h, j - 1)]), (s_dv, qs_done)))
                        for mc in range(2):
                            wt = [(s_ac, k_done)]
                            if mc == 0:
                                wt.append((s_dv, kv_dv.get((h, j - 2), 0)))
                            p.emit("tensor", lambda e, j=j, mc=mc, b=b, sl=sl:
                                   e.matmul(ps_kv[j % 2][:, :],
                                            ksb[b % 2][:, mc * 128:(mc + 1) * 128],
                                            vb[sl][:, (2 * j + mc) * 128:
                                                   (2 * j + mc + 1) * 128],
                                            start=(mc == 0), stop=(mc == 1)),
                                   wait=tuple(wt),
                                   inc=(s_pe, 1) if mc == 1 else None)
                        return s_pe.n

                    def EPI(j, b, ixu_done):
                        p.emit("scalar", lambda e, j=j, b=b, sl=sl: e.activation(
                            ats[sl][:, j * 256:(j + 1) * 256], ps_a[b % 2][:, :],
                            AF.Copy),
                            wait=((s_pe, ixu_done), (s_st, head_st.get(h - 2, 0))),
                            inc=(s_ac, 1))
                        evac_ac[b] = s_ac.n
                        p.emit("vector", lambda e, j=j, bd=bd: e.scalar_tensor_tensor(
                            kv32[(j + 1) % 2][:, :], kv32[j % 2][:, :], bd,
                            ps_kv[j % 2][:, :], ALU.mult, ALU.add),
                            wait=((s_pe, ixu_done),), inc=(s_dv, 1))
                        kv_dv[(h, j)] = s_dv.n
                        if j < NB - 1:
                            p.emit("vector", lambda e, j=j: e.tensor_copy(
                                kvbf[(j + 1) % 2][:, :], kv32[(j + 1) % 2][:, :]),
                                inc=(s_dv, 1))
                            kvbf_dv[(h, j)] = s_dv.n

                    LOOKAHEAD = True
                    if LOOKAHEAD:
                        pend = None
                        for j in range(NB):
                            b = bcnt
                            bcnt += 1
                            sd = S(j, b)
                            td = TR(j, b)
                            md = MASK(j, b, sd)
                            kd_ = KS(j, b, td)
                            if pend is not None:
                                ixu = IXU(*pend)
                                EPI(pend[0], pend[1], ixu)
                            pend = (j, b, md, kd_)
                        ixu = IXU(*pend)
                        EPI(pend[0], pend[1], ixu)
                    else:
                        for j in range(NB):
                            b = bcnt
                            bcnt += 1
                            sd = S(j, b)
                            td = TR(j, b)
                            md = MASK(j, b, sd)
                            kd_ = KS(j, b, td)
                            ixu = IXU(j, b, md, kd_)
                            EPI(j, b, ixu)

                    head_pe[h] = s_pe.n
                    head_dv[h] = s_dv.n
                    head_ac[h] = s_ac.n
                    qs_dv[h] = s_pe.n
                    p.emit("scalar", lambda e, h=h, sl=sl: e.dma_start(
                        out=self.at_img[:, h * T:(h + 1) * T], in_=ats[sl][:, :]),
                        inc=(s_st, 16))
                    p.emit("gpsimd", lambda e, h=h: e.dma_start(
                        out=self.u_in[h * HD:(h + 1) * HD, :], in_=kv32[0][:, :]),
                        wait=((s_dv, kv_dv[(h, NB - 1)]),), inc=(s_st, 16))
                    head_st[h] = s_st.n
                    if h + 2 < H:
                        emit_loads(h + 2)
        return s_st.n

    def collective(self, u_stores_done):
        p = self.p
        p.emit("gpsimd", lambda e: e.collective_compute(
            "AllGather", ALU.bypass, replica_groups=[list(range(NCORE))],
            ins=[self.u_in], outs=[self.u_out]),
            wait=((self.s_st, u_stores_done),), inc=(self.s_cc, 1))
        return self.s_cc.n

    def gate(self):
        p = self.p
        s_pe, s_ac, s_st, s_wl = self.s_pe, self.s_ac, self.s_st, self.s_wl
        sc = ExitStack()
        with sc:
            gsl = [self.sb(f"g_sl{i}", [128, KC * 128], BF16, sc) for i in range(2)]
            gst = [self.sb(f"g_st{i}", [128, T], BF16, sc) for i in range(2)]
            stage_free = {}

            def evac(oc, nt, bank, mm_done):
                sl = oc % 2
                wt = [(s_pe, mm_done)]
                if nt == 0:
                    wt.append((s_st, stage_free.get(sl, 0)))
                p.emit("scalar", lambda e, sl=sl, nt=nt, bank=bank: e.activation(
                    gst[sl][:, nt * NW:(nt + 1) * NW], bank[:, :], AF.Sigmoid),
                    wait=tuple(wt), inc=(s_ac, 1))
                if nt == NT - 1:
                    p.emit("scalar", lambda e, oc=oc, sl=sl: e.dma_start(
                        out=self.g_img[:, oc * T:(oc + 1) * T], in_=gst[sl][:, :]),
                        inc=(s_st, 16))
                    stage_free[sl] = s_st.n
                return (s_ac, s_ac.n)

            self.fmm(self.wg_img, KC, gsl, self.ln_ready, evac,
                     extra_slab_wait=self.fence())
        return s_st.n

    def assemble(self, cc_done, ps_ss, ps_corr):
        p = self.p
        s_pe, s_dv, s_ac, s_xl, s_st = (self.s_pe, self.s_dv, self.s_ac,
                                        self.s_xl, self.s_st)
        sc = ExitStack()
        with sc:
            atc = [self.sb(f"p5_at{i}", [128, T], BF16, sc) for i in range(2)]
            gc = [self.sb(f"p5_g{i}", [128, T], BF16, sc) for i in range(2)]
            qtc = [self.sb(f"p5_qt{i}", [128, T], BF16, sc) for i in range(2)]
            qdc = [self.sb(f"p5_qd{i}", [128, 256], F32, sc) for i in range(2)]
            qsc1 = self.sb("p5_qs", [128, T], BF16, sc)
            qsc = [qsc1, qsc1]
            ut = [self.sb(f"p5_ut{i}", [128, NCORE * 128], F32, sc) for i in range(2)]
            gnw1 = self.sb("p5_gnw", [128, T], BF16, sc)
            gnw = [gnw1, gnw1]
            kva = self.sb("p5_kva", [128, 128], F32, sc)
            kvb = self.sb("p5_kvb", [128, 128], F32, sc)
            kbf = [self.sb(f"p5_kbf{i}", [128, 128], BF16, sc) for i in range(2)]
            af = [self.sb(f"p5_af{i}", [128, 256], BF16, sc) for i in range(2)]
            sqb = [self.sb(f"p5_sq{i}", [128, 256], BF16, sc) for i in range(2)]

            fence = self.fence()
            z = self.zero_stats(ps_ss[0], ps_ss[1])
            loads = {}
            head_done = {}
            head_acd = {}
            head_ped = {}

            def emit_loads(h):
                sl = h % 2
                w = [(s_dv, head_done.get(h - 2, 0)),
                     (s_ac, head_acd.get(h - 2, 0)),
                     (s_pe, head_ped.get(h - 2, 0)),
                     (self.s_cc, cc_done)]
                if h < 2:
                    w += list(fence)
                for dst, src in ((atc[sl], self.at_img), (qtc[sl], self.qt_img)):
                    p.emit("sync", lambda e, dst=dst, src=src, h=h: e.dma_start(
                        out=dst[:, :], in_=src[:, h * T:(h + 1) * T]),
                        wait=tuple(w), inc=(s_xl, 16))
                p.emit("sync", lambda e, sl=sl, h=h: e.dma_start(
                    out=gc[sl][:, :], in_=self.g_img[:, h * T:(h + 1) * T]),
                    wait=tuple(w), inc=(s_xl, 16))
                p.emit("sync", lambda e, sl=sl, h=h: e.dma_start(
                    out=qdc[sl][:, :], in_=self.qd_img[:, h * 256:(h + 1) * 256]),
                    wait=tuple(w), inc=(s_xl, 16))
                for r in range(NCORE):
                    p.emit("sync", lambda e, sl=sl, h=h, r=r: e.dma_start(
                        out=ut[sl][:, r * 128:(r + 1) * 128],
                        in_=self.u_out[(r * H + h) * HD:(r * H + h + 1) * HD, :]),
                        wait=tuple(w), inc=(s_xl, 16))
                loads[h] = s_xl.n

            emit_loads(0)
            emit_loads(1)

            corr_free = {}
            af_free = {}
            for h in range(H):
                sl = h % 2
                bd = float(BD_TAB[h])
                ld = loads[h]
                p.emit("scalar", lambda e, sl=sl, h=h: e.activation(
                    gnw[sl][:, :], gc[sl][:, :], AF.Copy,
                    scale=self.nw_sb[:, h:h + 1]),
                    wait=((s_xl, ld), (s_dv, head_done.get(h - 1, 0))),
                    inc=(s_ac, 1))
                gnw_done = s_ac.n
                qd_rep = bass.AP(qdc[sl], 0, [[256, 128], [0, NB], [1, 256]])
                qt_v = bass.AP(qtc[sl], 0, [[T, 128], [256, NB], [1, 256]])
                qs_v = bass.AP(qsc[sl], 0, [[T, 128], [256, NB], [1, 256]])
                p.emit("vector", lambda e, qs_v=qs_v, qt_v=qt_v, qd_rep=qd_rep:
                       e.tensor_tensor(qs_v, qt_v, qd_rep, ALU.mult),
                       wait=((s_xl, ld), (s_pe, head_ped.get(h - 1, 0))),
                       inc=(s_dv, 1))
                qs_done = s_dv.n
                cur, nxt = kva, kvb
                p.emit("vector", lambda e, sl=sl, h=h, cur=cur: e.tensor_scalar(
                    cur[:, :], ut[sl][:, 0:128],
                    self.cc_sb[:, h:h + 1], None, ALU.mult),
                    wait=((s_xl, ld),))
                for r in range(1, NCORE):
                    p.emit("vector", lambda e, sl=sl, h=h, r=r, cur=cur:
                           e.scalar_tensor_tensor(
                               cur[:, :], ut[sl][:, r * 128:(r + 1) * 128],
                               self.cc_sb[:, r * H + h:r * H + h + 1],
                               cur[:, :], ALU.mult, ALU.add))
                p.emit("vector", lambda e, cur=cur: e.tensor_copy(
                    kbf[0][:, :], cur[:, :]),
                    wait=((s_pe, head_ped.get(h - 1, 0)),), inc=(s_dv, 1))
                kbf_done = {0: s_dv.n}
                corr_mms = {}

                for j in range(NB):
                    p.emit("tensor", lambda e, j=j, sl=sl: e.matmul(
                        ps_corr[j % 2][:, :], kbf[j % 2][:, :],
                        qsc[sl][:, j * 256:(j + 1) * 256],
                        start=True, stop=True),
                        wait=((s_dv, kbf_done[j]), (s_dv, qs_done),
                              (s_dv, corr_free.get(j % 2, 0))),
                        inc=(s_pe, 1))
                    corr_mm = s_pe.n
                    corr_mms[j] = corr_mm
                    if j < NB - 1:
                        p.emit("vector", lambda e, cur=cur, nxt=nxt, bd=bd:
                               e.tensor_scalar(
                                   nxt[:, :], cur[:, :], bd, None, ALU.mult))
                        p.emit("vector", lambda e, j=j, nxt=nxt: e.tensor_copy(
                            kbf[(j + 1) % 2][:, :], nxt[:, :]),
                            wait=((s_pe, corr_mms.get(j - 1, 0)),),
                            inc=(s_dv, 1))
                        kbf_done[j + 1] = s_dv.n
                        cur, nxt = nxt, cur
                    p.emit("vector", lambda e, j=j, sl=sl: e.tensor_add(
                        af[j % 2][:, :], ps_corr[j % 2][:, :],
                        atc[sl][:, j * 256:(j + 1) * 256]),
                        wait=((s_pe, corr_mm), (s_ac, af_free.get(j % 2, 0)),
                              (s_pe, af_free.get(("p", j % 2), 0))),
                        inc=(s_dv, 1))
                    af_done = s_dv.n
                    corr_free[j % 2] = af_done
                    p.emit("vector", lambda e, j=j, sl=sl, h=h: e.tensor_mul(
                        self.act_buf[:, h * T + j * 256: h * T + (j + 1) * 256],
                        af[j % 2][:, :], gnw[sl][:, j * 256:(j + 1) * 256]),
                        wait=((s_ac, gnw_done),), inc=(s_dv, 1))
                    p.emit("scalar", lambda e, j=j: e.activation(
                        sqb[j % 2][:, :], af[j % 2][:, :], AF.Square),
                        wait=((s_dv, af_done),
                              (s_pe, af_free.get(("p", j % 2), 0))),
                        inc=(s_ac, 1))
                    sq_done = s_ac.n
                    af_free[j % 2] = sq_done
                    p.emit("tensor", lambda e, j=j: e.matmul(
                        ps_ss[j // 4][32 * (j % 4):32 * (j % 4) + 1, 0:256],
                        self.onec_sb[:, :], sqb[j % 2][:, :],
                        start=False, stop=(h == H - 1), skip_group_check=True,
                        tile_position=(0, 32 * (j % 4))),
                        wait=((s_ac, sq_done), (s_dv, z)), inc=(self.s_pe, 1))
                    af_free[("p", j % 2)] = self.s_pe.n
                head_done[h] = s_dv.n
                head_acd[h] = s_ac.n
                head_ped[h] = s_pe.n
                if h + 2 < H:
                    emit_loads(h + 2)
            ss_done = self.s_pe.n
        return ss_done

    def zproj(self, rb, bc_done, ps_sum, ps_ssq, with_stats=True):
        p = self.p
        s_pe, s_dv, s_ac, s_xl, s_st = (self.s_pe, self.s_dv, self.s_ac,
                                        self.s_xl, self.s_st)
        sc = ExitStack()
        with sc:
            osl = [self.sb(f"z_sl{i}", [128, KC * 128], BF16, sc) for i in range(2)]
            xo = [self.sb(f"z_xo{i}", [128, T], F32, sc) for i in range(2)]
            xn = self.sb("z_xn", [128, T], F32, sc)
            tsb = self.sb("z_t", [128, NW], F32, sc)
            xb = self.sb("z_xb", [128, T], BF16, sc)
            sqb = self.sb("z_sq", [128, T], BF16, sc)

            z = self.zero_stats(ps_sum, ps_ssq)
            zfence = self.fence()
            xo_free = {}
            xn_free = {"st": 0}
            t_free = {}
            cast_free = {}
            pe_stats = {"n": 0}
            state = {"dv_nt": 0}

            def evac(oc, nt, bank, mm_done):
                sl = oc % 2
                if nt == 0:
                    xo_w = (xo_free.get(sl, (s_dv, 0)),)
                    if oc < 2:
                        xo_w = xo_w + zfence
                    p.emit("sync", lambda e, oc=oc, sl=sl: e.dma_start(
                        out=xo[sl][:, :],
                        in_=self.x1_img[:, oc * T:(oc + 1) * T]),
                        wait=xo_w, inc=(s_xl, 16))
                    xo_free["ld", sl] = s_xl.n
                p.emit("vector", lambda e, nt=nt, bank=bank: e.tensor_mul(
                    tsb[:, :], bank[:, :], rb["rstd_b"][:, nt * NW:(nt + 1) * NW]),
                    wait=((s_pe, mm_done), (s_dv, bc_done)))
                p.emit("vector", lambda e, nt=nt, sl=sl: e.tensor_add(
                    xn[:, nt * NW:(nt + 1) * NW], tsb[:, :],
                    xo[sl][:, nt * NW:(nt + 1) * NW]),
                    wait=((s_xl, xo_free.get(("ld", sl), 0)),
                          (s_st, xn_free["st"]), (s_ac, cast_free.get(0, 0))),
                    inc=(s_dv, 1))
                state["dv_nt"] = s_dv.n
                if nt == NT - 1:
                    dvd = s_dv.n
                    xo_free[sl] = (s_dv, dvd)
                    p.emit("sync", lambda e, oc=oc: e.dma_start(
                        out=self.x2_img[:, oc * T:(oc + 1) * T], in_=xn[:, :]),
                        wait=((s_dv, dvd),), inc=(s_st, 16))
                    xn_free["st"] = s_st.n
                    if with_stats:
                        p.emit("scalar", lambda e: e.activation(
                            xb[:, :], xn[:, :], AF.Copy),
                            wait=((s_dv, dvd), (s_pe, pe_stats["n"])))
                        p.emit("scalar", lambda e: e.activation(
                            sqb[:, :], xn[:, :], AF.Square), inc=(s_ac, 1))
                        cd = s_ac.n
                        cast_free[0] = cd
                        for nt2 in range(NT):
                            p.emit("tensor", self.stats_mm(
                                ps_sum, xb[:, nt2 * NW:(nt2 + 1) * NW], nt2,
                                oc == KC - 1),
                                wait=((s_ac, cd), (s_dv, z)))
                            p.emit("tensor", self.stats_mm(
                                ps_ssq, sqb[:, nt2 * NW:(nt2 + 1) * NW], nt2,
                                oc == KC - 1),
                                inc=(s_pe, 1) if nt2 == NT - 1 else None)
                        pe_stats["n"] = s_pe.n
                return (s_dv, state["dv_nt"])

            self.fmm(self.wo_img, KC, osl, self.y_ready, evac,
                     extra_slab_wait=self.fence())
        return pe_stats["n"], s_st.n

    def build(self):
        p = self.p
        self.alloc_globals()
        self.ffn_bank_free = {}

        scA = ExitStack()
        with scA:
            self.ps_mm = [self.psum(f"ps_mm{i}", [128, 512], F32, scA)
                          for i in range(2)]
            self.ps_bc = self.psum("ps_bc", [128, 512], F32, scA)
            ps_sum = self.psum("ps_sum", [128, 512], F32, scA)
            ps_ssq = self.psum("ps_ssq", [128, 512], F32, scA)

            sc0 = ExitStack()
            with sc0:
                R = self.mk_rows(sc0, rms=False)
                xb = self.sb("p0_xb", [128, T], BF16, sc0)
                sq = self.sb("p0_sq", [128, T], BF16, sc0)
                xf = [self.sb(f"p0_xf{i}", [128, T], F32, sc0) for i in range(2)]
                st_done = self.stats_pass(self.x0_img, ps_sum, ps_ssq, xb, sq)
                bc_done = self.rows_bcast(ps_sum, ps_ssq, st_done, R, self.ps_bc)
                self.ln_ready = self.normalize(self.x0_img, R, bc_done, xf)
                if self.tap == "ln1":
                    self.p.emit("gpsimd", lambda e: e.dma_start(
                        out=self.tapb_img, in_=self.act_buf[:, :]),
                        wait=((self.s_dv, self.ln_ready),), inc=(self.s_st, 16))
                    self.p.emit("gpsimd", lambda e: e.dma_start(
                        out=self.tap_img[:, 0:T], in_=R["mean_b"][:, :]),
                        wait=((self.s_dv, self.ln_ready),), inc=(self.s_st, 16))
                    self.p.emit("gpsimd", lambda e: e.dma_start(
                        out=self.tap_img[:, T:2 * T], in_=R["rstd_b"][:, :]),
                        wait=((self.s_dv, self.ln_ready),), inc=(self.s_st, 16))

            self.ffn_bank_free = {}
            st1_done, ffn1_st = self.ffn(self.x0_img, self.x1_img, True,
                                         ps_sum, ps_ssq, tag="f1")

            sc1 = ExitStack()
            with sc1:
                R = self.mk_rows(sc1, rms=False)
                xf = [self.sb(f"p1_xf{i}", [128, T], F32, sc1) for i in range(2)]
                bc_done = self.rows_bcast(ps_sum, ps_ssq, st1_done, R, self.ps_bc)
                self.ln_ready = self.normalize(self.x1_img, R, bc_done, xf,
                                               st_barrier=ffn1_st)

        scB = ExitStack()
        with scB:
            self.ps_mm = [self.psum(f"pb_mm{i}", [128, 512], F32, scB)
                          for i in range(2)]
            self.qkv_prod()

        u_done = self.attn_core()
        cc_done = self.collective(u_done)

        scD = ExitStack()
        with scD:
            self.ps_mm = [self.psum(f"pd_mm{i}", [128, 512], F32, scD)
                          for i in range(2)]
            self.gate()

        scE = ExitStack()
        with scE:
            rb = {}
            rb["rstd_b"] = self.sb("r_bcast", [128, T], F32, scE)
            scE2 = ExitStack()
            with scE2:
                psE = ExitStack()
                with psE:
                    self.ps_bc = self.psum("p5_bc", [128, 512], F32, psE)
                    ps_ss = [self.psum(f"p5_ss{i}", [128, 512], F32, psE)
                             for i in range(2)]
                    ps_corr = [self.psum(f"p5_c{i}", [128, 256], F32, psE)
                               for i in range(2)]
                    ss_done = self.assemble(cc_done, ps_ss, ps_corr)
                    rb["row_ssq"] = self.sb("p5_rssq", [1, T], F32, scE2)
                    rb["row_t"] = self.sb("p5_rt", [1, T], F32, scE2)
                    rb["row_hi"] = self.sb("p5_rhi", [1, T], BF16, scE2)
                    rb["row_lo"] = self.sb("p5_rlo", [1, T], BF16, scE2)
                    rb["row_ev"] = rb["rstd_b"][:, 0:1024]
                    self.p.emit("sync", None,
                                wait=((self.s_dv, self.s_dv.n),
                                      (self.s_ac, self.s_ac.n)))
                    bc_done = self.rows_bcast(ps_ss[0], ps_ss[1], ss_done, rb,
                                              self.ps_bc, rms=True)
            self.y_ready = self.s_dv.n

            scF = ExitStack()
            with scF:
                self.ps_mm = [self.psum(f"p6_mm{i}", [128, 512], F32, scF)
                              for i in range(2)]
                self.ps_bc = self.psum("p6_bc", [128, 512], F32, scF)
                ps_sum = self.psum("p6_sum", [128, 512], F32, scF)
                ps_ssq = self.psum("p6_ssq", [128, 512], F32, scF)
                st2_done, zst = self.zproj(rb, bc_done, ps_sum, ps_ssq)

                sc2 = ExitStack()
                with sc2:
                    R = self.mk_rows(sc2, rms=False)
                    xf = [self.sb(f"p7_xf{i}", [128, T], F32, sc2) for i in range(2)]
                    bc2 = self.rows_bcast(ps_sum, ps_ssq, st2_done, R, self.ps_bc)
                    self.ln_ready = self.normalize(self.x2_img, R, bc2, xf,
                                                   st_barrier=zst)

                self.ffn_bank_free = {}
                _, ffn2_st = self.ffn(self.x2_img, self.y_img, False, tag="f2")

        if self.tap:
            src = {"x1": self.x1_img, "x2": self.x2_img}.get(self.tap)
            srcb = {"ln2": None, "gate": self.g_img, "at": self.at_img,
                    "qt": self.qt_img, "kt": self.kt_img, "v": self.v_img}.get(
                        self.tap)
            if src is not None:
                p.emit("gpsimd", lambda e: e.dma_start(out=self.tap_img, in_=src),
                       wait=self.fence(), inc=(self.s_st, 16))
            if srcb is not None:
                p.emit("gpsimd", lambda e: e.dma_start(out=self.tapb_img, in_=srcb),
                       wait=self.fence(), inc=(self.s_st, 16))

        p.emit("sync", None, wait=((self.s_st, self.s_st.n),
                                   (self.s_wl, self.s_wl.n),
                                   (self.s_xl, self.s_xl.n)))
        p.emit("gpsimd", None, wait=((self.s_st, self.s_st.n),))

        self.p.replay()
        self.ctx.close()

        self.nc.all_engine_barrier()
        self.nc.clear_and_free_semaphores([s.h for s in self.p.sems])
        self.nc.all_engine_barrier()
        return self.nc

    _rows_id = 0

    def mk_rows(self, cx, rms=False):
        Builder._rows_id += 1
        i = Builder._rows_id
        R = {
            "row_sum": self.sb(f"r{i}_sum", [1, T], F32, cx),
            "row_ssq": self.sb(f"r{i}_ssq", [1, T], F32, cx),
            "row_t": self.sb(f"r{i}_t", [1, T], F32, cx),
            "row_hi": self.sb(f"r{i}_hi", [1, T], BF16, cx),
            "row_lo": self.sb(f"r{i}_lo", [1, T], BF16, cx),
            "mean_b": self.sb(f"r{i}_meanb", [128, T], F32, cx),
            "rstd_b": self.sb(f"r{i}_rstdb", [128, T], F32, cx),
        }
        R["row_ev"] = R["mean_b"][:, 0:1024]
        return R



def _stage_inputs(L0_attn_hidden_states, L0_attn_qkv_w, L0_attn_out_w,
                  L0_attn_gate_w, L0_attn_norm_w, L0_ffn_route_w, L0_ffn_A,
                  L0_ffn_Bw):
    x = np.asarray(L0_attn_hidden_states, dtype=np.float32)
    qkv_w = np.asarray(L0_attn_qkv_w, dtype=np.float32)
    out_w = np.asarray(L0_attn_out_w, dtype=np.float32)
    gate_w = np.asarray(L0_attn_gate_w, dtype=np.float32)
    norm_w = np.asarray(L0_attn_norm_w, dtype=np.float32)
    route_w = np.asarray(L0_ffn_route_w, dtype=np.float32)
    A = np.asarray(L0_ffn_A, dtype=np.float32)
    Bw = np.asarray(L0_ffn_Bw, dtype=np.float32)

    qkv3 = qkv_w.reshape(H, 3, HD, D)
    Wq = qkv3[:, 0].reshape(H * HD, D)
    Wk = qkv3[:, 1].reshape(H * HD, D)
    Wv = qkv3[:, 2].reshape(H * HD, D)

    shared = {
        "wq_img": _img_lhsT(Wq.T.copy()),
        "wk_img": _img_lhsT(Wk.T.copy()),
        "wv_img": _img_lhsT(Wv.T.copy(), ocw=512),
        "wg_img": _img_lhsT(gate_w.T.copy()),
        "wo_img": _img_lhsT(out_w.T.copy()),
        "a_img": _img_lhsT(A.reshape(ER, D).T.copy(), ocw=ER),
        "bw_img": np.ascontiguousarray(Bw.transpose(0, 2, 1).reshape(ER, D)),
        "rt_img": _img_lhsT(route_w.T.copy(), ocw=E),
        "nw_img": np.ascontiguousarray(norm_w.reshape(H, 128).T),
        "qd_img": np.ascontiguousarray(
            np.broadcast_to(QD_TAB[:, None, :], (H, 128, BS)
                            ).transpose(1, 0, 2).reshape(128, H * BS)),
        "kd_img": np.ascontiguousarray(
            KD_TAB.reshape(H, 2, 128).transpose(2, 0, 1).reshape(128, H * 2)),
        "mk_img": np.ascontiguousarray(
            MASKT_TAB.reshape(H, 2, 128, BS).transpose(2, 0, 1, 3
                                                       ).reshape(128, H * 2 * BS)),
        "em_img": np.ascontiguousarray(
            (np.arange(ER) // 16 == np.arange(E)[:, None]).astype(BFnp)),
        "id16": np.eye(128, dtype=BFnp),
        "idf32": np.eye(128, dtype=np.float32),
        "id8x": np.ascontiguousarray(
            np.tile(np.eye(32, 8, dtype=np.float32), (4, 1))),
        "onec": np.ones((128, 1), dtype=BFnp),
        "oner": np.ones((1, 128), dtype=BFnp),
    }

    in_maps = []
    for c in range(NCORE):
        b, q = c // 4, c % 4
        xs = x[b, q * T:(q + 1) * T, :]
        cc = np.zeros((NCORE, H), dtype=np.float32)
        for r in range(NCORE):
            rb_, rq = r // 4, r % 4
            if rb_ == b and rq < q:
                cc[r] = (BD_TAB.astype(np.float64) ** (NB * (q - 1 - rq))
                         ).astype(np.float32)
        m = dict(shared)
        m["x0_img"] = _img_act(xs)
        m["cc_img"] = np.ascontiguousarray(
            np.broadcast_to(cc.reshape(1, NCORE * H), (128, NCORE * H)))
        in_maps.append(m)
    return in_maps


def _unstage_output(res):
    outs = []
    for c in range(NCORE):
        img = np.asarray(res[c]["y_img"])
        xs = img.reshape(128, KC, T).transpose(1, 0, 2).reshape(D, T).T
        outs.append(xs)
    full = np.stack(outs, 0).reshape(2, 4 * T, D)
    return np.ascontiguousarray(full.astype(np.float32))


_BUILD_CACHE = {}


def kernel(**inputs):
    in_maps = _stage_inputs(**inputs)
    nc = Builder().build()
    res = run_bass_kernel_spmd(nc, in_maps, core_ids=list(range(NCORE)))
    return _unstage_output(res.results)
